# version 02
# baseline (speedup 1.0000x reference)
import numpy as np
import ml_dtypes
from contextlib import ExitStack

import concourse.bass as bass
import concourse.mybir as mybir
from concourse.bass_utils import run_bass_kernel_spmd

F32 = mybir.dt.float32
BF16 = mybir.dt.bfloat16
F16 = mybir.dt.float16
I32 = mybir.dt.int32
AF = mybir.ActivationFunctionType
ALU = mybir.AluOpType
AX = mybir.AxisListType

NC = 8
T = 2048
NTOK = NC * T
SEQ = 8192
D = 2048
BF = ml_dtypes.bfloat16

ENGINES = ("sync", "vector", "scalar", "tensor", "gpsimd")



def wt_tile(WT):
    K, M = WT.shape
    assert K % 128 == 0 and M % 128 == 0, (K, M)
    t = WT.reshape(K // 128, 128, M // 128, 128)
    t = np.ascontiguousarray(t.transpose(2, 1, 0, 3))
    return t.reshape(M // 128, 128, (K // 128) * 128)


def col_tile(v):
    M = v.shape[0]
    assert M % 128 == 0
    return np.ascontiguousarray(np.asarray(v).reshape(M // 128, 128).T)


def dmajor(x):
    return np.ascontiguousarray(np.asarray(x).T)


def shard_cols(x):
    x = np.asarray(x)
    return [np.ascontiguousarray(x[:, c * T:(c + 1) * T]) for c in range(NC)]


def unshard_cols(parts):
    return np.concatenate(parts, axis=1)


def shift_cols(x, nz=1):
    out = np.zeros_like(x)
    for s in range(2):
        seg = x[:, s * SEQ:(s + 1) * SEQ]
        out[:, s * SEQ + nz:(s + 1) * SEQ] = seg[:, : SEQ - nz]
    return out


def to_bf(a):
    return np.ascontiguousarray(a.astype(BF))



class P:
    def __init__(self, name):
        self.nc = bass.Bass("TRN2", target_bir_lowering=False, debug=False)
        self.name = name
        self.ctx = ExitStack()
        self.q = {e: [] for e in ENGINES}
        self.esem = {}
        self.ecnt = {e: 0 for e in ENGINES}
        self.dsem = {}
        self.dcnt = {e: 0 for e in ENGINES}

    def inp(self, name, shape, dt=F32):
        return self.nc.dram_tensor(name, list(shape), dt, kind="ExternalInput")

    def outp(self, name, shape, dt=F32):
        return self.nc.dram_tensor(name, list(shape), dt, kind="ExternalOutput")

    def dram(self, name, shape, dt=F32, shared=False):
        if shared:
            return self.nc.dram_tensor(name, list(shape), dt, addr_space="Shared")
        return self.nc.dram_tensor(name, list(shape), dt)

    def sb(self, name, shape, dt=F32):
        return self.ctx.enter_context(self.nc.sbuf_tensor(name, list(shape), dt))

    def ps(self, name, shape):
        return self.ctx.enter_context(self.nc.psum_tensor(name, list(shape), F32))

    def _sem(self, kind, e):
        d = self.esem if kind == "e" else self.dsem
        if e not in d:
            d[e] = self.ctx.enter_context(self.nc.semaphore(f"{kind}_{e}"))
        return d[e]

    def emit(self, eng, fn):
        self.q[eng].append(fn)

    def op(self, eng, fn, mark=False):
        if mark:
            sem = self._sem("e", eng)
            self.ecnt[eng] += 1
            v = self.ecnt[eng]
            self.emit(eng, lambda E, fn=fn, sem=sem: fn(E).then_inc(sem, 1))
            return (sem, v)
        self.emit(eng, lambda E, fn=fn: fn(E))
        return None

    def dma(self, eng, out, in_, **kw):
        sem = self._sem("d", eng)
        self.dcnt[eng] += 16
        v = self.dcnt[eng]
        self.emit(eng, lambda E, out=out, in_=in_, kw=kw, sem=sem:
                  E.dma_start(out=out, in_=in_, **kw).then_inc(sem, 16))
        return (sem, v)

    def wait(self, eng, *toks):
        for tok in toks:
            if tok is None:
                continue
            sem, v = tok
            self.emit(eng, lambda E, sem=sem, v=v: E.wait_ge(sem, v))

    def collective(self, kind, op, ins, outs, eng="gpsimd"):
        sem = self._sem("e", eng)
        self.ecnt[eng] += 1
        v = self.ecnt[eng]
        rg = [list(range(NC))]
        self.emit(eng, lambda E, kind=kind, op=op, ins=ins, outs=outs, rg=rg,
                  sem=sem: E.collective_compute(
                      kind, op, replica_groups=rg, ins=ins, outs=outs
                  ).then_inc(sem, 1))
        return (sem, v)

    def finish(self):
        with self.nc.Block() as blk:
            for e in ENGINES:
                fns = self.q[e]
                if not fns:
                    continue
                getattr(blk, e)(lambda E, fns=fns: [fn(E) for fn in fns])
        self.ctx.close()
        return self.nc


_PROGS = {}
_N_LAUNCH = [0]


def get_prog(key, builder, *args, **kw):
    if key not in _PROGS:
        p = P(key)
        builder(p, *args, **kw)
        _PROGS[key] = p.finish()
    return _PROGS[key]


def launch(nc, in_maps):
    _N_LAUNCH[0] += 1
    res = run_bass_kernel_spmd(nc, in_maps, core_ids=list(range(NC)))
    return res.results



def b_cast(p, nf, tile=16384):
    x = p.inp("x", [128, nf], F32)
    y = p.outp("y", [128, nf], BF16)
    nt = (nf + tile - 1) // tile
    bufs = [p.sb(f"c{i}", [128, tile], BF16) for i in range(2)]
    toks = [None, None]
    for i in range(nt):
        lo = i * tile
        w = min(tile, nf - lo)
        b = bufs[i % 2]
        if toks[i % 2] is not None:
            p.wait("gpsimd", toks[i % 2])
        t = p.dma("gpsimd", b[:, 0:w], x[:, lo:lo + w])
        p.wait("sync", t)
        toks[i % 2] = p.dma("sync", y[:, lo:lo + w], b[:, 0:w])
    p.wait("gpsimd", toks[0], toks[1])


def host_cast_all(arrs):
    flat = np.concatenate([np.ascontiguousarray(a).reshape(-1) for a in arrs])
    n = flat.size
    per = -(-n // (NC * 128)) * 128
    pad = np.zeros(NC * per, np.float32)
    pad[:n] = flat
    shards = pad.reshape(NC, 128, per // 128)
    nf = per // 128
    nc = get_prog(f"cast_{nf}", b_cast, nf)
    res = launch(nc, [{"x": np.ascontiguousarray(shards[c])} for c in range(NC)])
    out = np.concatenate([np.asarray(res[c]["y"]).reshape(-1) for c in range(NC)])[:n]
    outs = []
    o = 0
    for a in arrs:
        outs.append(np.ascontiguousarray(out[o:o + a.size].reshape(a.shape)))
        o += a.size
    return outs



def b_proj(p, K, M, epilogue="none", out_dt=F32, bias=False, addR=False,
           Tn=T, ksplit=1):
    nK = K // 128
    nM = M // 128
    nT = Tn // 512
    A = p.inp("A", [K, Tn], BF16)
    W = p.inp("W", [nM, 128, K], BF16)
    Y = p.outp("Y", [M, Tn], out_dt)
    Bc = p.inp("B", [128, nM], F32) if bias else None
    R = p.inp("R", [M, Tn], F32) if addR else None
    Yp = p.dram("Yp", [M, Tn], F32) if ksplit > 1 else None

    kper = nK // ksplit
    a_t = [p.sb(f"a{k}", [128, Tn], BF16) for k in range(kper)]
    w_t = [p.sb(f"w{j}", [128, kper * 128], BF16) for j in range(2)]
    stg = [p.sb(f"s{j}", [128, Tn], out_dt) for j in range(2)]
    need_r = addR or ksplit > 1
    rbuf = [p.sb(f"r{j}", [128, Tn], F32) for j in range(2)] if need_r else None
    bcol = p.sb("bc", [128, nM], F32) if bias else None
    psums = [p.ps(f"p{i}", [128, 512]) for i in range(4)]

    btok = p.dma("sync", bcol[:], Bc[:]) if bias else None
    wtok = [None, None]
    rtok = [None, None]
    sfree = [None, None]
    wfree = [None, None]
    edone = {}

    for half in range(ksplit):
        k0 = half * kper
        atoks = [p.dma("sync", a_t[k][:],
                       A[(k0 + k) * 128:(k0 + k + 1) * 128, :])
                 for k in range(kper)]
        last = (half == ksplit - 1)
        for mi in range(nM):
            it = half * nM + mi
            wb = it % 2
            sbn = it % 2
            if wfree[wb] is not None:
                p.wait("gpsimd", wfree[wb])
            wtok[wb] = p.dma("gpsimd", w_t[wb][:],
                             W[mi, :, k0 * 128:(k0 + kper) * 128])
            if (addR and last) or half > 0:
                if rtok[wb] is not None:
                    p.wait("gpsimd", edone[(it - 2, nT - 1)])
                src = R if (addR and last and ksplit == 1) else (
                    Yp if half > 0 else None)
                if ksplit > 1 and half > 0:
                    src = Yp
                elif addR and last:
                    src = R
                if src is not None:
                    rtok[wb] = p.dma("gpsimd", rbuf[wb][:],
                                     src[mi * 128:(mi + 1) * 128, :])
            p.wait("tensor", wtok[wb])
            if mi == 0:
                p.wait("tensor", *atoks)
            for tt in range(nT):
                ps = psums[(it * nT + tt) % len(psums)]
                prev = (it * nT + tt) - len(psums)
                if prev >= 0:
                    p.wait("tensor", edone[(prev // nT, prev % nT)])
                mmtok = None
                for ki in range(kper):
                    mmtok = p.op(
                        "tensor",
                        lambda E, ps=ps, wb=wb, ki=ki, tt=tt:
                        E.matmul(ps[:, :],
                                 w_t[wb][:, ki * 128:(ki + 1) * 128],
                                 a_t[ki][:, tt * 512:(tt + 1) * 512],
                                 start=(ki == 0), stop=(ki == kper - 1)),
                        mark=(ki == kper - 1))
                sb_ = stg[sbn]
                sl = (slice(None), slice(tt * 512, (tt + 1) * 512))
                if not last:
                    eng, mode = "vector", "partial"
                elif epilogue in ("sigmoid", "silu", "relusq"):
                    eng, mode = "scalar", epilogue
                else:
                    eng, mode = "vector", epilogue
                if tt == 0 and sfree[sbn] is not None:
                    p.wait(eng, sfree[sbn])
                p.wait(eng, mmtok)
                extra = (half > 0)
                if extra:
                    p.wait("vector", mmtok, rtok[wb])
                    addtok = p.op("vector", lambda E, ps=ps, wb=wb, sl=sl:
                                  E.tensor_tensor(out=ps[:, :], in0=ps[:, :],
                                                  in1=rbuf[wb][sl],
                                                  op=ALU.add), mark=True)
                    if eng != "vector":
                        p.wait(eng, addtok)
                if mode == "partial":
                    tok = p.op(eng, lambda E, ps=ps, sb_=sb_, sl=sl:
                               E.tensor_copy(out=sb_[sl], in_=ps[:, :]),
                               mark=True)
                elif mode == "none":
                    if bias:
                        p.wait(eng, btok)
                        tok = p.op(eng, lambda E, ps=ps, sb_=sb_, sl=sl, mi=mi:
                                   E.tensor_scalar(out=sb_[sl], in0=ps[:, :],
                                                   scalar1=bcol[:, mi:mi + 1],
                                                   scalar2=None, op0=ALU.add),
                                   mark=True)
                    else:
                        tok = p.op(eng, lambda E, ps=ps, sb_=sb_, sl=sl:
                                   E.tensor_copy(out=sb_[sl], in_=ps[:, :]),
                                   mark=True)
                elif mode == "addR":
                    p.wait(eng, rtok[wb])
                    if bias:
                        p.wait(eng, btok)
                        p.op(eng, lambda E, ps=ps, mi=mi:
                             E.tensor_scalar(out=ps[:, :], in0=ps[:, :],
                                             scalar1=bcol[:, mi:mi + 1],
                                             scalar2=None, op0=ALU.add))
                    tok = p.op(eng, lambda E, ps=ps, sb_=sb_, sl=sl, wb=wb:
                               E.tensor_tensor(out=sb_[sl], in0=ps[:, :],
                                               in1=rbuf[wb][sl], op=ALU.add),
                               mark=True)
                elif mode.startswith("scale:"):
                    f = float(mode.split(":")[1])
                    tok = p.op(eng, lambda E, ps=ps, sb_=sb_, sl=sl, f=f:
                               E.tensor_scalar(out=sb_[sl], in0=ps[:, :],
                                               scalar1=f, scalar2=None,
                                               op0=ALU.mult), mark=True)
                elif mode in ("sigmoid", "silu"):
                    fn = AF.Sigmoid if mode == "sigmoid" else AF.Silu
                    tok = p.op(eng, lambda E, ps=ps, sb_=sb_, sl=sl, fn=fn:
                               E.activation(out=sb_[sl], in_=ps[:, :], func=fn),
                               mark=True)
                elif mode == "relusq":
                    p.op(eng, lambda E, ps=ps:
                         E.activation(out=ps[:, :], in_=ps[:, :], func=AF.Relu))
                    tok = p.op(eng, lambda E, ps=ps, sb_=sb_, sl=sl:
                               E.activation(out=sb_[sl], in_=ps[:, :],
                                            func=AF.Square), mark=True)
                else:
                    raise ValueError(mode)
                edone[(it, tt)] = tok
            p.wait("sync", edone[(it, nT - 1)])
            dst = Y if last else Yp
            sfree[sbn] = p.dma("sync", dst[mi * 128:(mi + 1) * 128, :],
                               stg[sbn][:])
            wfree[wb] = edone[(it, nT - 1)]
    p.wait("gpsimd", sfree[0], sfree[1])


def run_proj(tag, A_bf, Wslabs, epilogue="none", out_np=np.float32,
             bias_cols=None, R=None, Tn=T, ksplit=1):
    K = A_bf[0].shape[0]
    nM = Wslabs.shape[0]
    M = nM * 128
    dt = F32 if out_np == np.float32 else BF16
    key = (f"proj_{K}_{M}_{epilogue}_{'bf' if dt == BF16 else 'f'}"
           f"_{bias_cols is not None}_{R is not None}_{Tn}_{ksplit}")
    nc = get_prog(key, b_proj, K, M, epilogue, dt, bias_cols is not None,
                  R is not None, Tn, ksplit)
    maps = []
    for c in range(NC):
        m = {"A": A_bf[c], "W": Wslabs}
        if bias_cols is not None:
            m["B"] = bias_cols
        if R is not None:
            m["R"] = R[c]
        maps.append(m)
    res = launch(nc, maps)
    return [np.asarray(res[c]["Y"]) for c in range(NC)]



def b_ln(p, eps, affine, out_f32, out_bf, Df=D, Tn=T):
    nD = Df // 128
    nT = Tn // 512
    x = p.inp("x", [Df, Tn], F32)
    wcol = p.inp("w", [128, nD], F32) if affine else None
    bcolI = p.inp("b", [128, nD], F32) if affine else None
    yf = p.outp("yf", [Df, Tn], F32) if out_f32 else None
    yb = p.outp("yb", [Df, Tn], BF16) if out_bf else None

    xt = [p.sb(f"x{k}", [128, Tn], F32) for k in range(nD)]
    sq = [p.sb(f"sq{j}", [128, 512], BF16) for j in range(2)]
    onesf = p.sb("onesf", [128, 1], F32)
    onesb = p.sb("onesb", [128, 1], BF16)
    onesr = p.sb("onesr", [1, 128], F32)
    rows = p.sb("rows", [1, 4 * 512], F32)
    brd1 = p.sb("brd1", [128, Tn], F32)
    brd2 = p.sb("brd2", [128, Tn], F32)
    wc = p.sb("wc", [128, nD], F32) if affine else None
    bc = p.sb("bc", [128, nD], F32) if affine else None
    stgf = [p.sb(f"sf{j}", [128, Tn], F32) for j in range(2)] if out_f32 else None
    stgb = [p.sb(f"sb{j}", [128, Tn], BF16) for j in range(2)] if out_bf else None
    tmp = p.sb("tmp", [128, Tn], F32)
    ps1 = p.ps("ps1", [1, 512])
    ps2 = p.ps("ps2", [1, 512])
    psb = [p.ps(f"psb{i}", [128, 512]) for i in range(2)]

    xtoks = [p.dma("sync", xt[k][:], x[k * 128:(k + 1) * 128, :])
             for k in range(nD)]
    if affine:
        wtok = p.dma("sync", wc[:], wcol[:])
        btok = p.dma("sync", bc[:], bcolI[:])
    epsc = p.sb("epsc", [1, 1], F32)
    m0 = p.op("vector", lambda E: E.memset(epsc[:], float(eps)), mark=True)
    m1 = p.op("vector", lambda E: E.memset(onesf[:], 1.0), mark=True)
    m2 = p.op("vector", lambda E: E.memset(onesb[:], 1.0), mark=True)
    m3 = p.op("vector", lambda E: E.memset(onesr[:], 1.0), mark=True)

    rowdone = {}
    brddone = {}
    for tt in range(nT):
        sl = slice(tt * 512, (tt + 1) * 512)
        p.wait("tensor", m1, m2, m3)
        if tt == 0:
            p.wait("tensor", *xtoks)
        if tt > 0:
            p.wait("tensor", rowdone[tt - 1])
        s1tok = None
        for k in range(nD):
            s1tok = p.op("tensor", lambda E, k=k, sl=sl:
                         E.matmul(ps1[:, :], onesf[:, :], xt[k][:, sl],
                                  start=(k == 0), stop=(k == nD - 1)),
                         mark=(k == nD - 1))
        s2tok = None
        sqdone = {}
        if tt == 0:
            p.wait("scalar", *xtoks)
        for k in range(nD):
            if k >= 2:
                p.wait("scalar", sqdone[k - 2])
            sqt = p.op("scalar", lambda E, k=k, sl=sl:
                       E.activation(out=sq[k % 2][:, :], in_=xt[k][:, sl],
                                    func=AF.Square), mark=True)
            p.wait("tensor", sqt)
            s2tok = p.op("tensor", lambda E, k=k:
                         E.matmul(ps2[:, :], onesb[:, :], sq[k % 2][:, :],
                                  start=(k == 0), stop=(k == nD - 1)),
                         mark=True)
            sqdone[k] = s2tok
        mu = rows[:, 0:512]
        s2m = rows[:, 512:1024]
        rstd = rows[:, 1024:1536]
        b2 = rows[:, 1536:2048]
        p.wait("vector", s1tok, s2tok)
        p.op("vector", lambda E, mu=mu: E.tensor_scalar(
            out=mu, in0=ps1[:, :], scalar1=1.0 / Df, scalar2=None, op0=ALU.mult))
        p.op("vector", lambda E, s2m=s2m: E.tensor_scalar(
            out=s2m, in0=ps2[:, :], scalar1=1.0 / Df, scalar2=None, op0=ALU.mult))
        p.op("vector", lambda E, mu=mu, s2m=s2m, rstd=rstd: E.tensor_tensor(
            out=rstd, in0=mu, in1=mu, op=ALU.mult))
        vtok = p.op("vector", lambda E, s2m=s2m, rstd=rstd: E.tensor_tensor(
            out=rstd, in0=s2m, in1=rstd, op=ALU.subtract), mark=True)
        p.wait("scalar", vtok)
        if tt == 0:
            p.wait("scalar", m0)
        p.op("scalar", lambda E, rstd=rstd: E.activation(
            out=rstd, in_=rstd, func=AF.Ln, bias=epsc[:, 0:1]))
        rtok = p.op("scalar", lambda E, rstd=rstd: E.activation(
            out=rstd, in_=rstd, func=AF.Exp, scale=-0.5), mark=True)
        p.wait("vector", rtok)
        rowdone[tt] = p.op("vector", lambda E, mu=mu, rstd=rstd, b2=b2:
                           E.scalar_tensor_tensor(out=b2, in0=mu, scalar=-1.0,
                                                  in1=rstd, op0=ALU.mult,
                                                  op1=ALU.mult), mark=True)
        p.wait("tensor", rowdone[tt])
        if tt >= 1:
            p.wait("tensor", brddone[tt - 1])
        bt1 = p.op("tensor", lambda E, rstd=rstd: E.matmul(
            psb[0][:, :], onesr[:, :], rstd, start=True, stop=True), mark=True)
        bt2 = p.op("tensor", lambda E, b2=b2: E.matmul(
            psb[1][:, :], onesr[:, :], b2, start=True, stop=True), mark=True)
        p.wait("scalar", bt1, bt2)
        e1 = p.op("scalar", lambda E, sl=sl: E.copy(out=brd1[:, sl],
                                                    in_=psb[0][:, :]), mark=True)
        e2 = p.op("scalar", lambda E, sl=sl: E.copy(out=brd2[:, sl],
                                                    in_=psb[1][:, :]), mark=True)
        brddone[tt] = e2
        p.wait("vector", e1, e2)
    p.wait("vector", *xtoks)
    stok = [None, None]
    for k in range(nD):
        j = k % 2
        if stok[j] is not None:
            p.wait("vector", stok[j])
        p.op("vector", lambda E, k=k: E.tensor_tensor(
            out=tmp[:], in0=xt[k][:], in1=brd1[:], op=ALU.mult))

        dsts = []
        if affine and k == 0:
            p.wait("vector", wtok, btok)
        if out_f32:
            if affine:
                p.op("vector", lambda E, k=k: E.tensor_tensor(
                    out=tmp[:], in0=tmp[:], in1=brd2[:], op=ALU.add))
                ftok = p.op("vector", lambda E, k=k, j=j: E.tensor_scalar(
                    out=stgf[j][:], in0=tmp[:], scalar1=wc[:, k:k + 1],
                    scalar2=bc[:, k:k + 1], op0=ALU.mult, op1=ALU.add),
                    mark=True)
            else:
                ftok = p.op("vector", lambda E, k=k, j=j: E.tensor_tensor(
                    out=stgf[j][:], in0=tmp[:], in1=brd2[:], op=ALU.add),
                    mark=True)
            p.wait("sync", ftok)
            t1 = p.dma("sync", yf[k * 128:(k + 1) * 128, :], stgf[j][:])
            stok[j] = t1
            if out_bf:
                btok2 = p.op("vector", lambda E, j=j: E.tensor_copy(
                    out=stgb[j][:], in_=stgf[j][:]), mark=True)
                p.wait("sync", btok2)
                stok[j] = p.dma("sync", yb[k * 128:(k + 1) * 128, :], stgb[j][:])
        else:
            if affine:
                p.op("vector", lambda E: E.tensor_tensor(
                    out=tmp[:], in0=tmp[:], in1=brd2[:], op=ALU.add))
                btok2 = p.op("vector", lambda E, k=k, j=j: E.tensor_scalar(
                    out=stgb[j][:], in0=tmp[:], scalar1=wc[:, k:k + 1],
                    scalar2=bc[:, k:k + 1], op0=ALU.mult, op1=ALU.add),
                    mark=True)
            else:
                btok2 = p.op("vector", lambda E, j=j: E.tensor_tensor(
                    out=stgb[j][:], in0=tmp[:], in1=brd2[:], op=ALU.add),
                    mark=True)
            p.wait("sync", btok2)
            stok[j] = p.dma("sync", yb[k * 128:(k + 1) * 128, :], stgb[j][:])
    p.wait("gpsimd", stok[0], stok[1])


def run_ln(tag, X, eps=1e-5, w=None, b=None, out_f32=True, out_bf=True,
           Df=D):
    affine = w is not None
    key = f"ln_{eps}_{affine}_{out_f32}_{out_bf}_{Df}"
    nc = get_prog(key, b_ln, eps, affine, out_f32, out_bf, Df)
    maps = []
    for c in range(NC):
        m = {"x": X[c]}
        if affine:
            m["w"] = col_tile(w)
            m["b"] = col_tile(b)
        maps.append(m)
    res = launch(nc, maps)
    outf = [np.asarray(res[c]["yf"]) for c in range(NC)] if out_f32 else None
    outb = [np.asarray(res[c]["yb"]) for c in range(NC)] if out_bf else None
    return outf, outb



def b_mix(p, nmix, sig_last):
    nD = D // 128
    h = p.inp("h", [D, T], F32)
    hs = p.inp("hs", [D, T], F32)
    mcols = p.inp("m", [128, nD * nmix], F32)
    outs = [p.outp(f"o{i}", [D, T], BF16) for i in range(nmix)]

    ht = [p.sb(f"h{j}", [128, T]) for j in range(2)]
    hst = [p.sb(f"hs{j}", [128, T]) for j in range(2)]
    dt_ = [p.sb(f"d{j}", [128, T]) for j in range(2)]
    ar = p.sb("ar", [128, T])
    stg = [[p.sb(f"st{i}_{j}", [128, T], BF16) for j in range(2)]
           for i in range(nmix)]
    mc = p.sb("mc", [128, nD * nmix], F32)
    mtok = p.dma("sync", mc[:], mcols[:])
    p.wait("vector", mtok)
    p.wait("scalar", mtok)

    htok = [None, None]
    hstok = [None, None]
    sdone = {}
    sigdone = {}
    for k in range(nD):
        j = k % 2
        if k >= 2:
            for i in range(nmix):
                p.wait("sync", sdone[(k - 2, i)])
        htok[j] = p.dma("sync", ht[j][:], h[k * 128:(k + 1) * 128, :])
        hstok[j] = p.dma("sync", hst[j][:], hs[k * 128:(k + 1) * 128, :])
        p.wait("vector", htok[j], hstok[j])
        p.op("vector", lambda E, j=j: E.tensor_tensor(
            out=dt_[j][:], in0=ht[j][:], in1=hst[j][:], op=ALU.subtract))
        for i in range(nmix):
            issig = sig_last and i == nmix - 1
            if issig:
                if k >= 1:
                    p.wait("vector", sigdone[k - 1])
                if k >= 2:
                    p.wait("scalar", sdone[(k - 2, i)])
                vt = p.op("vector", lambda E, j=j, k=k, i=i: E.scalar_tensor_tensor(
                    out=ar[:], in0=dt_[j][:], scalar=mc[:, i * nD + k:i * nD + k + 1],
                    in1=hst[j][:], op0=ALU.mult, op1=ALU.add), mark=True)
                p.wait("scalar", vt)
                tok = p.op("scalar", lambda E, j=j, i=i: E.activation(
                    out=stg[i][j][:], in_=ar[:], func=AF.Sigmoid), mark=True)
                sigdone[k] = tok
            else:
                if k >= 2:
                    p.wait("vector", sdone[(k - 2, i)])
                tok = p.op("vector", lambda E, j=j, k=k, i=i: E.scalar_tensor_tensor(
                    out=stg[i][j][:], in0=dt_[j][:],
                    scalar=mc[:, i * nD + k:i * nD + k + 1],
                    in1=hst[j][:], op0=ALU.mult, op1=ALU.add), mark=True)
            p.wait("sync", tok)
            sdone[(k, i)] = p.dma("sync", outs[i][k * 128:(k + 1) * 128, :],
                                  stg[i][j][:])
    for i in range(nmix):
        p.wait("gpsimd", sdone[(nD - 2, i)], sdone[(nD - 1, i)])


def run_mix(tag, H, Hs, mixes, sig_last):
    nmix = len(mixes)
    key = f"mix_{nmix}_{sig_last}"
    nc = get_prog(key, b_mix, nmix, sig_last)
    mcols = np.concatenate([col_tile(m) for m in mixes], axis=1)
    mcols = np.ascontiguousarray(mcols)
    res = launch(nc, [{"h": H[c], "hs": Hs[c], "m": mcols} for c in range(NC)])
    return [[np.asarray(res[c][f"o{i}"]) for c in range(NC)]
            for i in range(nmix)]



def b_scan(p, nrows=256, ntok=NTOK, tile=2048):
    nR = nrows // 128
    nt = ntok // tile
    seq_tiles = SEQ // tile
    k_in = p.inp("k", [nrows, ntok], F32)
    v_in = p.inp("v", [nrows, ntok], F32)
    ucol = p.inp("u", [128, nR], F32)
    wcol = p.inp("w", [128, nR], F32)
    out = p.outp("wkv", [nrows, ntok], F32)

    kt = [p.sb(f"k{j}", [128, tile]) for j in range(2)]
    vt = [p.sb(f"v{j}", [128, tile]) for j in range(2)]
    q = p.sb("q", [128, tile])
    t2 = p.sb("t2", [128, tile])
    e1 = p.sb("e1", [128, tile])
    e2 = p.sb("e2", [128, tile])
    bta = p.sb("bta", [128, tile])
    av = p.sb("av", [128, tile])
    bv = p.sb("bv", [128, tile])
    rb = p.sb("rb", [128, tile])
    stg = [p.sb(f"sg{j}", [128, tile]) for j in range(2)]
    uc = p.sb("uc", [128, nR], F32)
    wc = p.sb("wc", [128, nR], F32)
    nwc = p.sb("nwc", [128, nR], F32)
    st = p.sb("st", [128, 2], F32)

    ptok = p.dma("sync", uc[:], ucol[:])
    ptok2 = p.dma("sync", wc[:], wcol[:])
    p.wait("vector", ptok, ptok2)
    nwtok = p.op("vector", lambda E: E.tensor_scalar(
        out=nwc[:], in0=wc[:], scalar1=-1.0, scalar2=None, op0=ALU.mult),
        mark=True)
    p.wait("scalar", ptok, ptok2, nwtok)

    ktok = [None, None]
    vtok = [None, None]
    sdone = [None, None]
    escan_tok = [None]
    stok_ = [None]
    for r in range(nR):
        for i in range(nt):
            j = i % 2
            lo = i * tile
            if sdone[j] is not None:
                p.wait("sync", sdone[j])
            ktok[j] = p.dma("sync", kt[j][:], k_in[r * 128:(r + 1) * 128,
                                                   lo:lo + tile])
            vtok[j] = p.dma("sync", vt[j][:], v_in[r * 128:(r + 1) * 128,
                                                   lo:lo + tile])
            if i % seq_tiles == 0:
                p.op("vector", lambda E: E.memset(st[:], 0.0))
            p.wait("vector", ktok[j])
            p.op("vector", lambda E, j=j, r=r: E.tensor_scalar(
                out=q[:], in0=kt[j][:], scalar1=uc[:, r:r + 1],
                scalar2=wc[:, r:r + 1], op0=ALU.add, op1=ALU.max))
            qtok = p.op("vector", lambda E, j=j: E.tensor_tensor(
                out=t2[:], in0=kt[j][:], in1=q[:], op=ALU.subtract),
                mark=True)
            p.wait("scalar", qtok)
            if escan_tok[0] is not None:
                p.wait("scalar", escan_tok[0])
            p.op("scalar", lambda E, r=r: E.activation(
                out=e1[:], in_=q[:], func=AF.Exp, bias=nwc[:, r:r + 1],
                scale=-1.0))
            etok = p.op("scalar", lambda E, j=j, r=r: E.activation(
                out=e2[:], in_=t2[:], func=AF.Exp, bias=uc[:, r:r + 1]),
                mark=True)
            p.wait("vector", etok, vtok[j])
            p.op("vector", lambda E, j=j: E.tensor_tensor(
                out=bta[:], in0=e2[:], in1=vt[j][:], op=ALU.mult))
            if stok_[0] is not None:
                p.wait("vector", stok_[0])
            avtok = p.op("vector", lambda E: E.tensor_tensor_scan(
                out=av[:], data0=e1[:], data1=bta[:], initial=st[:, 0:1],
                op0=ALU.mult, op1=ALU.add), mark=True)
            p.wait("vector", avtok)
            p.op("vector", lambda E: E.tensor_copy(out=st[:, 0:1],
                                                   in_=av[:, tile - 1:tile]))
            escan_tok[0] = p.op("vector", lambda E: E.tensor_tensor_scan(
                out=bv[:], data0=e1[:], data1=e2[:], initial=st[:, 1:2],
                op0=ALU.mult, op1=ALU.add), mark=True)
            p.wait("vector", escan_tok[0])
            stok_[0] = p.op("vector", lambda E: E.tensor_copy(
                out=st[:, 1:2], in_=bv[:, tile - 1:tile]), mark=True)
            p.op("vector", lambda E: E.reciprocal(out=rb[:], in_=bv[:]))
            if sdone[j] is not None:
                p.wait("vector", sdone[j])
            wtok2 = p.op("vector", lambda E, j=j: E.tensor_tensor(
                out=stg[j][:], in0=av[:], in1=rb[:], op=ALU.mult), mark=True)
            p.wait("sync", wtok2)
            sdone[j] = p.dma("sync", out[r * 128:(r + 1) * 128, lo:lo + tile],
                             stg[j][:])
    p.wait("gpsimd", sdone[0], sdone[1])


def run_scan(Kd, Vd, u, w):
    nc = get_prog("scan", b_scan)
    maps = []
    for c in range(NC):
        sl = slice(c * 256, (c + 1) * 256)
        maps.append({
            "k": np.ascontiguousarray(Kd[sl]),
            "v": np.ascontiguousarray(Vd[sl]),
            "u": col_tile(u[sl]),
            "w": col_tile(w[sl]),
        })
    res = launch(nc, maps)
    return np.concatenate([np.asarray(res[c]["wkv"]) for c in range(NC)], axis=0)



def b_ew(p, kind, R, Tn=T):
    nR = R // 128
    a = p.inp("a", [R, Tn], F32)
    b = p.inp("b", [R, Tn], F32)
    x = p.inp("x", [R, Tn], F32) if kind == "combine" else None
    odt = BF16 if kind == "mulbf" else F32
    o = p.outp("o", [R, Tn], odt)
    at = [p.sb(f"a{j}", [128, Tn]) for j in range(2)]
    bt = [p.sb(f"b{j}", [128, Tn]) for j in range(2)]
    xt = [p.sb(f"x{j}", [128, Tn]) for j in range(2)] if kind == "combine" else None
    stg = [p.sb(f"s{j}", [128, Tn], odt) for j in range(2)]
    sdone = [None, None]
    for k in range(nR):
        j = k % 2
        if sdone[j] is not None:
            p.wait("sync", sdone[j])
        t1 = p.dma("sync", at[j][:], a[k * 128:(k + 1) * 128, :])
        t2 = p.dma("sync", bt[j][:], b[k * 128:(k + 1) * 128, :])
        toks = [t1, t2]
        if kind == "combine":
            toks.append(p.dma("sync", xt[j][:], x[k * 128:(k + 1) * 128, :]))
        p.wait("vector", *toks)
        if sdone[j] is not None:
            p.wait("vector", sdone[j])
        if kind == "mulbf":
            vt = p.op("vector", lambda E, j=j: E.tensor_tensor(
                out=stg[j][:], in0=at[j][:], in1=bt[j][:], op=ALU.mult),
                mark=True)
        else:
            p.op("vector", lambda E, j=j: E.tensor_tensor(
                out=at[j][:], in0=at[j][:], in1=bt[j][:], op=ALU.mult))
            vt = p.op("vector", lambda E, j=j: E.tensor_tensor(
                out=stg[j][:], in0=xt[j][:], in1=at[j][:], op=ALU.add),
                mark=True)
        p.wait("sync", vt)
        sdone[j] = p.dma("sync", o[k * 128:(k + 1) * 128, :], stg[j][:])
    p.wait("gpsimd", sdone[0], sdone[1])


def run_ew(kind, A, B, X=None, R=D):
    key = f"ew_{kind}_{R}"
    nc = get_prog(key, b_ew, kind, R)
    maps = []
    for c in range(NC):
        m = {"a": A[c], "b": B[c]}
        if X is not None:
            m["x"] = X[c]
        maps.append(m)
    res = launch(nc, maps)
    return [np.asarray(res[c]["o"]) for c in range(NC)]



def prep_weights(I):
    names_d2 = [
        ("l0_wk", I["L0_attn_tm_Wk"]), ("l0_wv", I["L0_attn_tm_Wv"]),
        ("l0_wr", I["L0_attn_tm_Wr"]), ("l0_wo", I["L0_attn_tm_Wo"]),
        ("l0_cwk", I["L0_attn_cm_Wk"]), ("l0_cwv", I["L0_attn_cm_Wv"]),
        ("l0_cwr", I["L0_attn_cm_Wr"]),
        ("l1_wq", I["L1_attn_wq"]), ("l1_wk", I["L1_attn_wk"]),
        ("l1_wv", I["L1_attn_wv"]), ("l1_wo", I["L1_attn_wo"]),
        ("l3_wq", I["L3_attn_w_q"]), ("l3_wo", I["L3_attn_w_o"]),
    ]
    tiled = []
    keys = []
    for nme, W in names_d2:
        tiled.append(wt_tile(np.asarray(W).T))
        keys.append(nme)
    tiled.append(wt_tile(np.pad(np.asarray(I["L2_attn_in_proj_w"]).T,
                                ((0, 0), (0, 64)))))
    keys.append("l2_inproj")
    tiled.append(wt_tile(np.asarray(I["L2_attn_out_proj_w"]).T))
    keys.append("l2_outproj")
    tiled.append(wt_tile(np.asarray(I["L3_attn_w_dkv"]).T))
    keys.append("l3_wdkv")
    tiled.append(np.ascontiguousarray(np.asarray(I["L3_attn_w_uk"]).T))
    keys.append("l3_wukT_nat")
    tiled.append(np.ascontiguousarray(np.asarray(I["L3_attn_w_uv"]).T))
    keys.append("l3_wuvT_nat")
    tiled.append(np.ascontiguousarray(np.asarray(I["L1_attn_wv"]).T))
    keys.append("l1_wvT_nat")
    outs = host_cast_all(tiled)
    WB = dict(zip(keys, outs))
    WB["l1_wr"] = host_ternw(wt_tile(np.asarray(I["L1_ffn_weight"]).T),
                             wt_tile(np.asarray(I["L1_ffn_rram_variation"]).T),
                             0.2, 0.6)
    WB["l3_wfc"] = host_l3tern(wt_tile(np.asarray(I["L3_ffn_w_fc"]).T),
                               5120 * 2048)
    WB["l3_wproj"] = host_l3tern(wt_tile(np.asarray(I["L3_ffn_w_proj"]).T),
                                 5120 * 2048)
    WB["l0_gu"] = host_fp4_dequant(np.asarray(I["L0_ffn_gate_up_packed"]),
                                   np.asarray(I["L0_ffn_gate_up_scales"]), "a")
    WB["l0_dn"] = host_fp4_dequant(np.asarray(I["L0_ffn_down_packed"]),
                                   np.asarray(I["L0_ffn_down_scales"]), "b")
    WB["l2_gu"] = host_fp4_dequant(np.asarray(I["L2_ffn_gate_up_packed"]),
                                   np.asarray(I["L2_ffn_gate_up_scales"]), "c")
    WB["l2_dn"] = host_fp4_dequant(np.asarray(I["L2_ffn_down_packed"]),
                                   np.asarray(I["L2_ffn_down_scales"]), "d")
    return WB



def kernel(**inputs):
    _N_LAUNCH[0] = 0
    I = {k: np.asarray(v) for k, v in inputs.items()}
    x_full = I["L0_attn_x"].reshape(NTOK, D).astype(np.float32, copy=False)
    X = shard_cols(dmajor(x_full))
    WB = prep_weights(I)
    X = layer0(I, X, WB)
    X = layer1(I, X, WB)
    X = layer2(I, X, WB)
    X = layer3(I, X, WB)
    out = unshard_cols(X)
    return np.ascontiguousarray(out.T).reshape(2, SEQ, D).astype(np.float32)



def b_proj_tm(p, K, M, Tn=T):
    nK = K // 128
    nTb = Tn // 128
    nMt = M // 512
    A = p.inp("A", [K, Tn], BF16)
    W = p.inp("W", [K, M], BF16)
    Y = p.outp("Y", [Tn, M], BF16)

    a_t = [p.sb(f"a{k}", [128, Tn], BF16) for k in range(nK)]
    w_t = [p.sb(f"w{k}", [128, M], BF16) for k in range(nK)]
    stg = [p.sb(f"s{j}", [128, M], BF16) for j in range(2)]
    psums = [p.ps(f"p{i}", [128, 512]) for i in range(4)]

    atoks = [p.dma("sync", a_t[k][:], A[k * 128:(k + 1) * 128, :])
             for k in range(nK)]
    wtoks = [p.dma("gpsimd", w_t[k][:], W[k * 128:(k + 1) * 128, :])
             for k in range(nK)]
    p.wait("tensor", *atoks)
    p.wait("tensor", *wtoks)
    sfree = [None, None]
    edone = {}
    for tb in range(nTb):
        for mt in range(nMt):
            idx = tb * nMt + mt
            ps = psums[idx % 4]
            if idx >= 4:
                p.wait("tensor", edone[idx - 4])
            mm = None
            for ki in range(nK):
                mm = p.op("tensor", lambda E, ps=ps, ki=ki, tb=tb, mt=mt:
                          E.matmul(ps[:, :],
                                   a_t[ki][:, tb * 128:(tb + 1) * 128],
                                   w_t[ki][:, mt * 512:(mt + 1) * 512],
                                   start=(ki == 0), stop=(ki == nK - 1)),
                          mark=(ki == nK - 1))
            p.wait("vector", mm)
            if mt == 0 and sfree[tb % 2] is not None:
                p.wait("vector", sfree[tb % 2])
            edone[idx] = p.op("vector", lambda E, ps=ps, tb=tb, mt=mt:
                              E.tensor_copy(
                                  out=stg[tb % 2][:, mt * 512:(mt + 1) * 512],
                                  in_=ps[:, :]), mark=True)
        p.wait("sync", edone[tb * nMt + nMt - 1])
        sfree[tb % 2] = p.dma("sync", Y[tb * 128:(tb + 1) * 128, :],
                              stg[tb % 2][:])
    p.wait("gpsimd", sfree[0], sfree[1])


def run_proj_tm(A_bf, WT_bf, M):
    K = A_bf[0].shape[0]
    key = f"projtm_{K}_{M}"
    nc = get_prog(key, b_proj_tm, K, M)
    res = launch(nc, [{"A": A_bf[c], "W": WT_bf} for c in range(NC)])
    return [np.asarray(res[c]["Y"]) for c in range(NC)]



def b_ternx(p, thr, scale, Df=D, Tn=T):
    nD = Df // 128
    x = p.inp("x", [Df, Tn], F32)
    o = p.outp("o", [Df, Tn], BF16)
    xt = [p.sb(f"x{j}", [128, Tn]) for j in range(2)]
    m1 = p.sb("m1", [128, Tn])
    m2 = p.sb("m2", [128, Tn])
    stg = [p.sb(f"s{j}", [128, Tn], BF16) for j in range(2)]
    sdone = [None, None]
    for k in range(nD):
        j = k % 2
        if sdone[j] is not None:
            p.wait("sync", sdone[j])
        t = p.dma("sync", xt[j][:], x[k * 128:(k + 1) * 128, :])
        p.wait("vector", t)
        p.op("vector", lambda E, j=j: E.tensor_scalar(
            out=m1[:], in0=xt[j][:], scalar1=float(thr), scalar2=None,
            op0=ALU.is_gt))
        p.op("vector", lambda E, j=j: E.tensor_scalar(
            out=m2[:], in0=xt[j][:], scalar1=float(-thr), scalar2=None,
            op0=ALU.is_lt))
        p.op("vector", lambda E: E.tensor_tensor(
            out=m1[:], in0=m1[:], in1=m2[:], op=ALU.subtract))
        if sdone[j] is not None:
            p.wait("vector", sdone[j])
        vt = p.op("vector", lambda E, j=j: E.tensor_scalar(
            out=stg[j][:], in0=m1[:], scalar1=float(scale), scalar2=None,
            op0=ALU.mult), mark=True)
        p.wait("sync", vt)
        sdone[j] = p.dma("sync", o[k * 128:(k + 1) * 128, :], stg[j][:])
    p.wait("gpsimd", sdone[0], sdone[1])


def run_ternx(X, thr, scale, Df=D):
    key = f"ternx_{thr}_{scale}_{Df}"
    nc = get_prog(key, b_ternx, thr, scale, Df)
    res = launch(nc, [{"x": X[c]} for c in range(NC)])
    return [np.asarray(res[c]["o"]) for c in range(NC)]



def b_rmsq(p, eps, Df=D, Tn=T):
    nD = Df // 128
    nT = Tn // 512
    ins = [p.inp("q", [Df, Tn], F32), p.inp("k", [Df, Tn], F32)]
    outs = [p.outp("qn", [Df, Tn], BF16), p.outp("kn", [Df, Tn], BF16)]
    xt = [p.sb(f"x{j}", [128, Tn]) for j in range(2)]
    sq = p.sb("sq", [128, Tn], BF16)
    onesb = p.sb("onesb", [128, 1], BF16)
    onesr = p.sb("onesr", [1, 128], F32)
    rows = p.sb("rows", [1, Tn], F32)
    epsc = p.sb("epsc", [1, 1], F32)
    stg = [p.sb(f"s{j}", [128, Tn], BF16) for j in range(2)]
    ps2 = p.ps("ps2", [1, 512])
    psb = [p.ps(f"psb{i}", [128, 512]) for i in range(2)]

    m0 = p.op("vector", lambda E: E.memset(epsc[:], float(eps)), mark=True)
    m2 = p.op("vector", lambda E: E.memset(onesb[:], 1.0), mark=True)
    m3 = p.op("vector", lambda E: E.memset(onesr[:], 1.0), mark=True)
    p.wait("scalar", m0)
    sdone = [None, None]
    brdprev = {}
    for ti in range(2):
        for g in range(nD):
            j = g % 2
            if sdone[j] is not None:
                p.wait("sync", sdone[j])
            t = p.dma("sync", xt[j][:], ins[ti][g * 128:(g + 1) * 128, :])
            p.wait("scalar", t)
            sqt = p.op("scalar", lambda E, j=j: E.activation(
                out=sq[:], in_=xt[j][:], func=AF.Square), mark=True)
            p.wait("tensor", sqt, m2, m3)
            rowtoks = []
            for tt in range(nT):
                sl = slice(tt * 512, (tt + 1) * 512)
                mm = p.op("tensor", lambda E, sl=sl: E.matmul(
                    ps2[:, :], onesb[:, :], sq[:, sl], start=True, stop=True),
                    mark=True)
                p.wait("vector", mm)
                vt = p.op("vector", lambda E, sl=sl: E.tensor_scalar(
                    out=rows[:, sl], in0=ps2[:, :], scalar1=1.0 / 128,
                    scalar2=None, op0=ALU.mult), mark=True)
                p.wait("tensor", vt)
                rowtoks.append(vt)
            p.wait("scalar", *rowtoks)
            p.op("scalar", lambda E: E.activation(
                out=rows[:, :], in_=rows[:, :], func=AF.Ln, bias=epsc[:, 0:1]))
            rt = p.op("scalar", lambda E: E.activation(
                out=rows[:, :], in_=rows[:, :], func=AF.Exp, scale=-0.5),
                mark=True)
            p.wait("tensor", rt)
            for tt in range(nT):
                sl = slice(tt * 512, (tt + 1) * 512)
                if (g, tt) != (0, 0) or ti > 0:
                    prev = brdprev.get(tt % 2)
                    if prev is not None:
                        p.wait("tensor", prev)
                bt = p.op("tensor", lambda E, sl=sl, tt=tt: E.matmul(
                    psb[tt % 2][:, :], onesr[:, :], rows[:, sl],
                    start=True, stop=True), mark=True)
                p.wait("vector", bt)
                if tt == 0 and sdone[j] is not None:
                    p.wait("vector", sdone[j])
                et = p.op("vector", lambda E, j=j, sl=sl, tt=tt: E.tensor_tensor(
                    out=stg[j][:, sl], in0=xt[j][:, sl], in1=psb[tt % 2][:, :],
                    op=ALU.mult), mark=True)
                brdprev[tt % 2] = et
            p.wait("sync", et)
            sdone[j] = p.dma("sync", outs[ti][g * 128:(g + 1) * 128, :],
                             stg[j][:])
    p.wait("gpsimd", sdone[0], sdone[1])


def run_rmsq(Q, Kf, eps=1.1920929e-07):
    key = f"rmsq_{eps}"
    nc = get_prog(key, b_rmsq, eps)
    res = launch(nc, [{"q": Q[c], "k": Kf[c]} for c in range(NC)])
    return ([np.asarray(res[c]["qn"]) for c in range(NC)],
            [np.asarray(res[c]["kn"]) for c in range(NC)])



def b_l1attn(p, npair=2, S=SEQ):
    nblkS = S // 128
    nTt = S // 512
    qns = [p.inp(f"qn{i}", [256, S], BF16) for i in range(npair)]
    kns = [p.inp(f"kn{i}", [256, S], BF16) for i in range(npair)]
    vs = [p.inp(f"v{i}", [S, 256], BF16) for i in range(npair)]
    lam = p.inp("lam", [128, 8], F32)
    masks = p.inp("masks", [128, 4 * 512], BF16)
    ys = [p.outp(f"y{i}", [256, S], BF16) for i in range(npair)]

    kn_t = [p.sb(f"skn{s}", [128, S], BF16) for s in range(2)]
    qn_t = [p.sb(f"sqn{s}", [128, S], BF16) for s in range(2)]
    v_sb = p.sb("vsb", [128, nblkS * 256], BF16)
    msk = p.sb("msk", [128, 4 * 512], BF16)
    Pb = [p.sb(f"Pb{j}", [128, 512], BF16) for j in range(2)]
    Pm = [p.sb(f"Pm{j}", [128, 512], BF16) for j in range(2)]
    onesb = p.sb("onesb", [128, 1], BF16)
    onesc = p.sb("onesc", [128, 1], F32)
    onesr = p.sb("onesr", [1, 128], F32)
    lamt = p.sb("lamt", [128, 8], F32)
    lrow = p.sb("lrow", [1, 4], F32)
    nlam = p.sb("nlam", [128, 1], F32)
    rl = p.sb("rl", [1, 512], F32)
    Bsb = p.sb("Bsb", [128, 512], F32)
    y1t = [p.sb(f"y1_{i}", [128, 512], F32) for i in range(2)]
    y2t = p.sb("y2t", [128, 512], F32)
    stg = [p.sb(f"st{j}", [128, 512], BF16) for j in range(4)]
    psP = [p.ps(f"psP{j}", [128, 512]) for j in range(2)]
    psO = [p.ps(f"psO{i}", [128, 512]) for i in range(2)]
    psL = p.ps("psL", [1, 512])
    psB = p.ps("psB", [128, 512])

    mtok = p.dma("sync", msk[:], masks[:])
    ltok = p.dma("sync", lamt[:], lam[:])
    m1 = p.op("vector", lambda E: E.memset(onesb[:], 1.0), mark=True)
    m1b = p.op("vector", lambda E: E.memset(onesc[:], 1.0), mark=True)
    m2 = p.op("vector", lambda E: E.memset(onesr[:], 1.0), mark=True)
    p.wait("vector", ltok)

    def vop(fn):
        tok = p.op("vector", fn, mark=True)
        p.wait("vector", tok)
        return tok
    vop(lambda E: E.tensor_tensor(out=lamt[:, 0:2], in0=lamt[:, 0:2],
                                  in1=lamt[:, 2:4], op=ALU.mult))
    lp = vop(lambda E: E.tensor_tensor(out=lamt[:, 4:6], in0=lamt[:, 4:6],
                                       in1=lamt[:, 6:8], op=ALU.mult))
    p.wait("tensor", lp, m1b)
    red = p.op("tensor", lambda E: E.matmul(
        psB[0:1, 0:2], onesc[:, :], lamt[:, 0:2], start=True,
        stop=True), mark=True)
    red2 = p.op("tensor", lambda E: E.matmul(
        psB[0:1, 4:6], onesc[:, :], lamt[:, 4:6], start=True,
        stop=True), mark=True)
    p.wait("vector", red, red2)
    vop(lambda E: E.tensor_reduce(out=lrow[:, 0:1], in_=psB[0:1, 0:2],
                                  axis=AX.X, op=ALU.add))
    rd2 = vop(lambda E: E.tensor_reduce(out=lrow[:, 1:2], in_=psB[0:1, 4:6],
                                  axis=AX.X, op=ALU.add))
    p.wait("scalar", rd2)
    se = p.op("scalar", lambda E: E.activation(
        out=lrow[:, 2:4], in_=lrow[:, 0:2], func=AF.Exp), mark=True)
    p.wait("scalar", se)
    p.wait("vector", se)
    vop(lambda E: E.tensor_tensor(out=lrow[:, 0:1], in0=lrow[:, 3:4],
                                  in1=lrow[:, 2:3], op=ALU.subtract))
    lf = vop(lambda E: E.tensor_scalar(
        out=lrow[:, 0:1], in0=lrow[:, 0:1], scalar1=0.8, scalar2=None,
        op0=ALU.subtract))
    p.wait("tensor", lf, m2)
    nb = p.op("tensor", lambda E: E.matmul(
        psB[:, 0:1], onesr[:, :], lrow[:, 0:1], start=True, stop=True),
        mark=True)
    p.wait("scalar", nb)
    nbc = p.op("scalar", lambda E: E.copy(out=nlam[:], in_=psB[:, 0:1]),
               mark=True)
    p.wait("vector", nbc, m1, mtok)
    p.wait("scalar", *[None])

    prev_last_etok = None
    prev_rl = None
    prev_y = {0: None, 1: None}
    prev_bsb = None
    prev_bsb_cons = None
    prev_pair_sdone = None
    first_btok = [True]
    for i in range(npair):
        lt0 = p.dma("sync", kn_t[0][:], kns[i][0:128, :])
        lt1 = p.dma("sync", kn_t[1][:], kns[i][128:256, :])
        lt2 = p.dma("sync", qn_t[0][:], qns[i][0:128, :])
        lt3 = p.dma("sync", qn_t[1][:], qns[i][128:256, :])
        vsrc = vs[i][:].rearrange("(b p) d -> p b d", p=128)
        lt4 = p.dma("sync", v_sb[:].rearrange("p (b d) -> p b d", d=256), vsrc)
        p.wait("tensor", lt0, lt1, lt2, lt3, lt4)
        ptoks = {}
        etoks = {}
        pvtoks = {}
        sdone = {}
        for tt in range(nTt):
            nblk = 4 * (tt + 1)
            for s in range(2):
                for b in range(nblk):
                    jj = b % 2
                    key = (s, tt, b - 2)
                    if key in etoks:
                        p.wait("tensor", etoks[key])
                    elif prev_last_etok is not None:
                        p.wait("tensor", prev_last_etok)
                    ptoks[(s, tt, b)] = p.op(
                        "tensor", lambda E, s=s, tt=tt, b=b, jj=jj:
                        E.matmul(psP[jj][:, :],
                                 kn_t[s][:, b * 128:(b + 1) * 128],
                                 qn_t[s][:, tt * 512:(tt + 1) * 512],
                                 start=True, stop=True), mark=True)
                    p.wait("scalar", ptoks[(s, tt, b)])
                    pvk = (s, tt, b - 2)
                    if pvk in pvtoks:
                        p.wait("scalar", pvtoks[pvk])
                    etoks[(s, tt, b)] = p.op(
                        "scalar", lambda E, jj=jj:
                        E.activation(out=Pb[jj][:, :], in_=psP[jj][:, :],
                                     func=AF.Exp, scale=0.08838834764831845),
                        mark=True)
                    prev_last_etok = etoks[(s, tt, b)]
                    diag = b >= 4 * tt
                    src_P = Pb[jj]
                    if diag:
                        r = b - 4 * tt
                        p.wait("vector", etoks[(s, tt, b)])
                        if pvk in pvtoks:
                            p.wait("vector", pvtoks[pvk])
                        mk2 = p.op("vector", lambda E, jj=jj, r=r:
                                   E.tensor_tensor(
                                       out=Pm[jj][:, :], in0=Pb[jj][:, :],
                                       in1=msk[:, r * 512:(r + 1) * 512],
                                       op=ALU.mult), mark=True)
                        p.wait("tensor", mk2)
                        src_P = Pm[jj]
                    else:
                        p.wait("tensor", etoks[(s, tt, b)])
                    if b == 0 and prev_rl is not None:
                        p.wait("tensor", prev_rl)
                    p.op("tensor", lambda E, src_P=src_P, b=b, nblk=nblk:
                         E.matmul(psL[:, :], onesb[:, :], src_P[:, :],
                                  start=(b == 0), stop=(b == nblk - 1)))
                    for dvo in range(2):
                        if b == 0 and prev_y[dvo] is not None:
                            p.wait("tensor", prev_y[dvo])
                        pvtoks[(s, tt, b)] = p.op(
                            "tensor",
                            lambda E, src_P=src_P, b=b, dvo=dvo, nblk=nblk:
                            E.matmul(psO[dvo][:, :],
                                     v_sb[:, b * 256 + dvo * 128:
                                          b * 256 + dvo * 128 + 128],
                                     src_P[:, :],
                                     start=(b == 0), stop=(b == nblk - 1)),
                            mark=(dvo == 1))
                lastmm = pvtoks[(s, tt, nblk - 1)]
                p.wait("vector", lastmm)
                rltok = p.op("vector", lambda E: E.reciprocal(
                    out=rl[:, :], in_=psL[:, :]), mark=True)
                p.wait("vector", rltok)
                p.wait("tensor", rltok)
                if first_btok[0]:
                    p.wait("tensor", nbc)
                    first_btok[0] = False
                else:
                    p.wait("tensor", prev_bsb)
                btok = p.op("tensor", lambda E: E.matmul(
                    psB[:, :], onesr[:, :], rl[:, :], start=True, stop=True),
                    mark=True)
                p.wait("scalar", btok)
                if prev_bsb_cons is not None:
                    p.wait("scalar", prev_bsb_cons)
                bsb = p.op("scalar", lambda E: E.copy(
                    out=Bsb[:, :], in_=psB[:, :]), mark=True)
                prev_bsb = bsb
                p.wait("vector", bsb)
                ev = {"rl": rltok, "y": {}}
                for dvo in range(2):
                    if s == 0:
                        ev["y"][dvo] = p.op(
                            "vector", lambda E, dvo=dvo: E.tensor_tensor(
                                out=y1t[dvo][:, :], in0=psO[dvo][:, :],
                                in1=Bsb[:, :], op=ALU.mult), mark=True)
                    else:
                        y2tok = p.op(
                            "vector", lambda E, dvo=dvo: E.tensor_tensor(
                                out=y2t[:, :], in0=psO[dvo][:, :],
                                in1=Bsb[:, :], op=ALU.mult), mark=True)
                        sb_ = stg[(tt * 2 + dvo) % 4]
                        sk = (tt * 2 + dvo - 4)
                        if sk >= 0:
                            p.wait("vector", sdone[sk])
                        elif prev_pair_sdone is not None:
                            p.wait("vector", prev_pair_sdone)
                        ev["y"][dvo] = p.op(
                            "vector", lambda E, dvo=dvo, sb_=sb_:
                            E.scalar_tensor_tensor(
                                out=sb_[:, :], in0=y2t[:, :],
                                scalar=nlam[:, 0:1], in1=y1t[dvo][:, :],
                                op0=ALU.mult, op1=ALU.add), mark=True)
                        p.wait("sync", ev["y"][dvo])
                        sdone[tt * 2 + dvo] = p.dma(
                            "sync",
                            ys[i][dvo * 128:(dvo + 1) * 128,
                                  tt * 512:(tt + 1) * 512], sb_[:, :])
                prev_rl = rltok
                prev_y = ev["y"]
                prev_bsb_cons = ev["y"][1]
        p.wait("sync", *[sdone[kk] for kk in sorted(sdone)[-4:]])
        prev_pair_sdone = sdone[sorted(sdone)[-1]]
        if i < npair - 1:
            p.wait("sync", prev_last_etok)
    p.wait("gpsimd", *[sdone[kk] for kk in sorted(sdone)[-4:]])


def run_l1attn(qn_full, kn_full, vtm_full, lq1, lk1, lq2, lk2):
    key = "l1attn"
    nc = get_prog(key, b_l1attn)
    lam = np.zeros((128, 8), np.float32)
    lam[:, 0] = lq1[:128]; lam[:, 1] = lq1[128:]
    lam[:, 2] = lk1[:128]; lam[:, 3] = lk1[128:]
    lam[:, 4] = lq2[:128]; lam[:, 5] = lq2[128:]
    lam[:, 6] = lk2[:128]; lam[:, 7] = lk2[128:]
    mm = np.zeros((128, 4 * 512), np.float32)
    for r in range(4):
        tk = r * 128 + np.arange(128)[:, None]
        tq = np.arange(512)[None, :]
        mm[:, r * 512:(r + 1) * 512] = (tk <= tq)
    maps = []
    for c in range(NC):
        m = {"lam": lam, "masks": to_bf(mm)}
        for i in range(2):
            pair = 2 * c + i
            b = pair // 8
            h = pair % 8
            rows = slice(256 * h, 256 * h + 256)
            cols = slice(b * SEQ, (b + 1) * SEQ)
            m[f"qn{i}"] = np.ascontiguousarray(qn_full[rows, cols])
            m[f"kn{i}"] = np.ascontiguousarray(kn_full[rows, cols])
            m[f"v{i}"] = np.ascontiguousarray(vtm_full[cols, rows])
        maps.append(m)
    res = launch(nc, maps)
    y_full = np.zeros((2048, NTOK), BF)
    for c in range(NC):
        for i in range(2):
            pair = 2 * c + i
            b = pair // 8
            h = pair % 8
            y_full[256 * h:256 * h + 256, b * SEQ:(b + 1) * SEQ] = \
                np.asarray(res[c][f"y{i}"])
    return y_full



def b_ternw(p, F_, thr, scale, with_r=True):
    w = p.inp("w", [128, F_], F32)
    r = p.inp("r", [128, F_], F32) if with_r else None
    o = p.outp("o", [128, F_], BF16)
    tile = 4096
    nt = (F_ + tile - 1) // tile
    wt_ = [p.sb(f"w{j}", [128, tile]) for j in range(2)]
    rt = [p.sb(f"r{j}", [128, tile]) for j in range(2)] if with_r else None
    m1 = p.sb("m1", [128, tile])
    m2 = p.sb("m2", [128, tile])
    stg = [p.sb(f"s{j}", [128, tile], BF16) for j in range(2)]
    sdone = [None, None]
    for i in range(nt):
        j = i % 2
        lo = i * tile
        wd = min(tile, F_ - lo)
        if sdone[j] is not None:
            p.wait("sync", sdone[j])
        t1 = p.dma("sync", wt_[j][:, 0:wd], w[:, lo:lo + wd])
        toks = [t1]
        if with_r:
            toks.append(p.dma("sync", rt[j][:, 0:wd], r[:, lo:lo + wd]))
        p.wait("vector", *toks)
        p.op("vector", lambda E, j=j, wd=wd: E.tensor_scalar(
            out=m1[:, 0:wd], in0=wt_[j][:, 0:wd], scalar1=float(thr),
            scalar2=None, op0=ALU.is_gt))
        p.op("vector", lambda E, j=j, wd=wd: E.tensor_scalar(
            out=m2[:, 0:wd], in0=wt_[j][:, 0:wd], scalar1=float(-thr),
            scalar2=None, op0=ALU.is_lt))
        p.op("vector", lambda E, wd=wd: E.tensor_tensor(
            out=m1[:, 0:wd], in0=m1[:, 0:wd], in1=m2[:, 0:wd],
            op=ALU.subtract))
        if sdone[j] is not None:
            p.wait("vector", sdone[j])
        if with_r:
            vt = p.op("vector", lambda E, j=j, wd=wd: E.scalar_tensor_tensor(
                out=stg[j][:, 0:wd], in0=m1[:, 0:wd], scalar=float(scale),
                in1=rt[j][:, 0:wd], op0=ALU.mult, op1=ALU.add), mark=True)
        else:
            vt = p.op("vector", lambda E, j=j, wd=wd: E.tensor_scalar(
                out=stg[j][:, 0:wd], in0=m1[:, 0:wd], scalar1=float(scale),
                scalar2=None, op0=ALU.mult), mark=True)
        p.wait("sync", vt)
        sdone[j] = p.dma("sync", o[:, lo:lo + wd], stg[j][:, 0:wd])
    p.wait("gpsimd", sdone[0], sdone[1])


def host_ternw(WT_tiled, RT_tiled, thr, scale):
    nM, _, Kd = WT_tiled.shape
    per = -(-nM // NC)
    F_ = per * Kd
    key = f"ternw_{F_}_{thr}_{scale}_{RT_tiled is not None}"
    nc = get_prog(key, b_ternw, F_, thr, scale, RT_tiled is not None)
    maps = []
    for c in range(NC):
        sl = list(range(c * per, min((c + 1) * per, nM)))
        wsl = np.zeros((per, 128, Kd), np.float32)
        wsl[:len(sl)] = WT_tiled[sl]
        m = {"w": np.ascontiguousarray(
            wsl.transpose(1, 0, 2).reshape(128, F_))}
        if RT_tiled is not None:
            rsl = np.zeros((per, 128, Kd), np.float32)
            rsl[:len(sl)] = RT_tiled[sl]
            m["r"] = np.ascontiguousarray(
                rsl.transpose(1, 0, 2).reshape(128, F_))
        maps.append(m)
    res = launch(nc, maps)
    out = np.zeros((NC * per, 128, Kd), BF)
    for c in range(NC):
        out[c * per:(c + 1) * per] = np.asarray(
            res[c]["o"]).reshape(128, per, Kd).transpose(1, 0, 2)
    return np.ascontiguousarray(out[:nM])


def layer1(I, X, WB):
    XN_f, XN_b = run_ln("top1", X, eps=1e-5)
    Q = run_proj("q", XN_b, WB["l1_wq"])
    Kf = run_proj("k", XN_b, WB["l1_wk"])
    Vtm = run_proj_tm(XN_b, WB["l1_wvT_nat"], 2048)
    Qn, Kn = run_rmsq(Q, Kf)
    qn_full = unshard_cols(Qn)
    kn_full = unshard_cols(Kn)
    vtm_full = np.ascontiguousarray(np.concatenate(Vtm, axis=0))
    y_full = run_l1attn(qn_full, kn_full, vtm_full,
                        np.asarray(I["L1_attn_lq1"]),
                        np.asarray(I["L1_attn_lk1"]),
                        np.asarray(I["L1_attn_lq2"]),
                        np.asarray(I["L1_attn_lk2"]))
    Y = shard_cols(y_full)
    S = run_proj("s1", Y, WB["l1_wo"], epilogue="scale:0.2")
    Xq = run_ternx(XN_f, 0.33, 0.5)
    G = run_proj("g1", Xq, WB["l1_wr"], epilogue="silu")
    return run_ew("combine", S, G, X=X)



def b_l3attn(p, npair=4, S=SEQ, slope=0.0625):
    nblkS = S // 128
    nTt = S // 512
    qs = [p.inp(f"q{i}", [128, S], BF16) for i in range(npair)]
    hsel = p.inp("hsel", [1, npair], I32)
    wuk = [p.inp(f"wuk{i}", [128, 128], BF16) for i in range(npair)]
    wuv = [p.inp(f"wuv{i}", [128, 128], BF16) for i in range(npair)]
    ckv = p.inp("ckv", [128, S], BF16)
    alib = p.inp("alib", [128, nblkS], F32)
    srow = p.inp("srow", [1, S], F32)
    masks = p.inp("masks", [128, 4 * 512], BF16)
    ys = [p.outp(f"y{i}", [128, S], BF16) for i in range(npair)]

    ckv_sb = p.sb("sckv", [128, S], BF16)
    q_sb = p.sb("qsb", [128, S], BF16)
    kfm = p.sb("kfm", [128, S], BF16)
    vsb = p.sb("vsb", [128, S], BF16)
    wk_sb = p.sb("wk", [128, 128], BF16)
    wv_sb = p.sb("wv", [128, 128], BF16)
    msk = p.sb("msk", [128, 4 * 512], BF16)
    alib_sb = p.sb("alibs", [128, nblkS], F32)
    srow_sb = p.sb("srows", [1, S], F32)
    sqb = p.sb("sqb", [128, 512], BF16)
    norm2 = p.sb("norm2", [1, S], F32)
    RB = p.sb("RB", [1, S], BF16)
    rrow = p.sb("rrow", [1, S], F32)
    km = p.sb("km", [1, 4], F32)
    Pb = [p.sb(f"Pb{j}", [128, 512], BF16) for j in range(2)]
    Pm = [p.sb(f"Pm{j}", [128, 512], BF16) for j in range(2)]
    onesb = p.sb("onesb", [128, 1], BF16)
    onesr1 = p.sb("onesr1", [1, 128], BF16)
    onesrf = p.sb("onesrf", [1, 128], F32)
    rl = p.sb("rl", [1, 512], F32)
    Bsb = p.sb("Bsb", [128, 512], F32)
    stg = [p.sb(f"st{j}", [128, 512], BF16) for j in range(4)]
    psP = [p.ps(f"psP{j}", [128, 512]) for j in range(2)]
    psO = p.ps("psO", [128, 512])
    psL = p.ps("psL", [1, 512])
    psB = p.ps("psB", [128, 512])
    psKV = [p.ps(f"psKV{j}", [128, 512]) for j in range(2)]

    mtok = p.dma("sync", msk[:], masks[:])
    atok = p.dma("sync", alib_sb[:], alib[:])
    stok = p.dma("sync", srow_sb[:], srow[:])
    ctok = p.dma("sync", ckv_sb[:], ckv[:])
    m1 = p.op("vector", lambda E: E.memset(onesb[:], 1.0), mark=True)
    m2 = p.op("vector", lambda E: E.memset(onesr1[:], 1.0), mark=True)
    m2f = p.op("vector", lambda E: E.memset(onesrf[:], 1.0), mark=True)
    p.wait("vector", mtok, atok, stok)

    prev = {"etok": None, "rl": None, "y": None, "bsb": None, "bsbc": None,
            "pair_s": None, "kv": None}
    sdone = {}
    for i in range(npair):
        lt0 = p.dma("sync", q_sb[:], qs[i][:])
        lt1 = p.dma("sync", wk_sb[:], wuk[i][:])
        lt2 = p.dma("sync", wv_sb[:], wuv[i][:])
        p.wait("tensor", lt1, lt2, ctok if i == 0 else None)
        kevs = []
        sqtoks = {}
        for tb in range(nTt):
            jj = tb % 2
            if tb >= 2:
                p.wait("tensor", sqtoks[tb - 2])
            if tb == 0 and prev["rl"] is not None:
                p.wait("tensor", prev["rl"])
                prev["rl"] = None
            kmm = p.op("tensor", lambda E, tb=tb, jj=jj: E.matmul(
                psKV[jj][:, :], wk_sb[:, :],
                ckv_sb[:, tb * 512:(tb + 1) * 512], start=True, stop=True),
                mark=True)
            p.wait("scalar", kmm)
            kev = p.op("scalar", lambda E, tb=tb, jj=jj: E.copy(
                out=kfm[:, tb * 512:(tb + 1) * 512], in_=psKV[jj][:, :]),
                mark=True)
            sq = p.op("scalar", lambda E, tb=tb, jj=jj: E.activation(
                out=sqb[:, :], in_=psKV[jj][:, :], func=AF.Square), mark=True)
            sqtoks[tb] = sq
            p.wait("tensor", sq, m1)
            nm = p.op("tensor", lambda E, tb=tb: E.matmul(
                psL[:, :], onesb[:, :], sqb[:, :], start=True, stop=True),
                mark=True)
            p.wait("vector", nm)
            nv = p.op("vector", lambda E, tb=tb: E.tensor_copy(
                out=norm2[:, tb * 512:(tb + 1) * 512], in_=psL[:, :]),
                mark=True)
            p.wait("tensor", nv)
            p.wait("scalar", nv)
            kevs.append((kev, nv))
        p.wait("vector", kevs[-1][1])

        def vop(fn):
            tok = p.op("vector", fn, mark=True)
            p.wait("vector", tok)
            return tok
        vop(lambda E: E.tensor_reduce(out=km[:, 0:1], in_=norm2[:, :],
                                      axis=AX.X, op=ALU.max))
        kmt = vop(lambda E: E.tensor_copy(out=km[:, 1:2], in_=km[:, 0:1]))
        p.wait("scalar", kmt)
        kms = p.op("scalar", lambda E: E.activation(
            out=km[:, 2:3], in_=km[:, 1:2], func=AF.Sqrt), mark=True)
        p.wait("scalar", kms)
        vevs = []
        for blk in range(nblkS):
            jj = blk % 2
            vmm = p.op("tensor", lambda E, blk=blk, jj=jj: E.matmul(
                psKV[jj][:, 0:128], ckv_sb[:, blk * 128:(blk + 1) * 128],
                wv_sb[:, :], start=True, stop=True), mark=True)
            p.wait("scalar", vmm)
            if blk >= 2:
                p.wait("scalar", vevs[blk - 2])
            vev = p.op("scalar", lambda E, blk=blk, jj=jj: E.copy(
                out=vsb[:, blk * 128:(blk + 1) * 128], in_=psKV[jj][:, 0:128]),
                mark=True)
            vevs.append(vev)
            p.wait("tensor", vev)
        p.wait("scalar", lt0)
        qevs = []
        for tb in range(nTt):
            sq = p.op("scalar", lambda E, tb=tb: E.activation(
                out=sqb[:, :], in_=q_sb[:, tb * 512:(tb + 1) * 512],
                func=AF.Square), mark=True)
            p.wait("tensor", sq)
            nm = p.op("tensor", lambda E: E.matmul(
                psL[:, :], onesb[:, :], sqb[:, :], start=True, stop=True),
                mark=True)
            p.wait("vector", nm)
            nv = p.op("vector", lambda E, tb=tb: E.tensor_copy(
                out=rrow[:, tb * 512:(tb + 1) * 512], in_=psL[:, :]),
                mark=True)
            p.wait("tensor", nv)
            p.wait("scalar", nv)
            qevs.append(nv)
        p.wait("scalar", qevs[-1])
        qs_t = p.op("scalar", lambda E: E.activation(
            out=rrow[:, :], in_=rrow[:, :], func=AF.Sqrt), mark=True)
        p.wait("scalar", qs_t)
        p.wait("vector", qs_t, kms)
        vop(lambda E: E.tensor_scalar(
            out=rrow[:, :], in0=rrow[:, :], scalar1=km[0:1, 2:3],
            scalar2=-1.0, op0=ALU.mult, op1=ALU.mult))
        rbt = vop(lambda E: E.scalar_tensor_tensor(
            out=RB[:, :], in0=srow_sb[:, :], scalar=-11.313708498984761,
            in1=rrow[:, :], op0=ALU.mult, op1=ALU.add))
        p.wait("tensor", rbt, m2)
        ptoks = {}
        etoks = {}
        pvtoks = {}
        for tt in range(nTt):
            nblk = 4 * (tt + 1)
            for b in range(nblk):
                jj = b % 2
                key = (tt, b - 2)
                if key in etoks:
                    p.wait("tensor", etoks[key])
                elif prev["etok"] is not None:
                    p.wait("tensor", prev["etok"])
                    prev["etok"] = None
                p.op("tensor", lambda E, b=b, tt=tt, jj=jj: E.matmul(
                    psP[jj][:, :], kfm[:, b * 128:(b + 1) * 128],
                    q_sb[:, tt * 512:(tt + 1) * 512], start=True, stop=False))
                ptoks[(tt, b)] = p.op("tensor", lambda E, tt=tt, jj=jj:
                                      E.matmul(
                    psP[jj][:, :], onesr1[:, :],
                    RB[:, tt * 512:(tt + 1) * 512], start=False, stop=True),
                    mark=True)
                p.wait("scalar", ptoks[(tt, b)])
                pvk = (tt, b - 2)
                if pvk in pvtoks:
                    p.wait("scalar", pvtoks[pvk])
                etoks[(tt, b)] = p.op("scalar", lambda E, jj=jj, b=b:
                                      E.activation(
                    out=Pb[jj][:, :], in_=psP[jj][:, :], func=AF.Exp,
                    scale=0.08838834764831845, bias=alib_sb[:, b:b + 1]),
                    mark=True)
                last_et = etoks[(tt, b)]
                diag = b >= 4 * tt
                src_P = Pb[jj]
                if diag:
                    r = b - 4 * tt
                    p.wait("vector", etoks[(tt, b)])
                    if pvk in pvtoks:
                        p.wait("vector", pvtoks[pvk])
                    mk2 = p.op("vector", lambda E, jj=jj, r=r: E.tensor_tensor(
                        out=Pm[jj][:, :], in0=Pb[jj][:, :],
                        in1=msk[:, r * 512:(r + 1) * 512], op=ALU.mult),
                        mark=True)
                    p.wait("tensor", mk2)
                    src_P = Pm[jj]
                else:
                    p.wait("tensor", etoks[(tt, b)])
                if b == 0 and prev["rl"] is not None:
                    p.wait("tensor", prev["rl"])
                p.op("tensor", lambda E, src_P=src_P, b=b, nblk=nblk:
                     E.matmul(psL[:, :], onesb[:, :], src_P[:, :],
                              start=(b == 0), stop=(b == nblk - 1)))
                if b == 0 and prev["y"] is not None:
                    p.wait("tensor", prev["y"])
                pvtoks[(tt, b)] = p.op(
                    "tensor", lambda E, src_P=src_P, b=b, nblk=nblk:
                    E.matmul(psO[:, :], vsb[:, b * 128:(b + 1) * 128],
                             src_P[:, :], start=(b == 0),
                             stop=(b == nblk - 1)), mark=True)
            lastmm = pvtoks[(tt, nblk - 1)]
            p.wait("vector", lastmm)
            rltok = p.op("vector", lambda E: E.reciprocal(
                out=rl[:, :], in_=psL[:, :]), mark=True)
            p.wait("vector", rltok)
            p.wait("tensor", rltok)
            if prev["bsb"] is not None:
                p.wait("tensor", prev["bsb"])
            if tt == 0:
                p.wait("tensor", m2f)
            btok = p.op("tensor", lambda E: E.matmul(
                psB[:, :], onesrf[:, :], rl[:, :],
                start=True, stop=True), mark=True)
            p.wait("scalar", btok)
            if prev["bsbc"] is not None:
                p.wait("scalar", prev["bsbc"])
            bsb = p.op("scalar", lambda E: E.copy(
                out=Bsb[:, :], in_=psB[:, :]), mark=True)
            prev["bsb"] = bsb
            p.wait("vector", bsb)
            sb_ = stg[tt % 4]
            sk = tt - 4
            if sk >= 0:
                p.wait("vector", sdone[(i, sk)])
            elif prev["pair_s"] is not None:
                p.wait("vector", prev["pair_s"])
            ytok = p.op("vector", lambda E, sb_=sb_: E.tensor_tensor(
                out=sb_[:, :], in0=psO[:, :], in1=Bsb[:, :], op=ALU.mult),
                mark=True)
            prev["y"] = ytok
            prev["rl"] = rltok
            prev["bsbc"] = ytok
            p.wait("sync", ytok)
            sdone[(i, tt)] = p.dma("sync",
                                   ys[i][:, tt * 512:(tt + 1) * 512],
                                   sb_[:, :])
        prev["etok"] = last_et
        p.wait("sync", *[sdone[(i, tt)] for tt in range(nTt - 4, nTt)])
        prev["pair_s"] = sdone[(i, nTt - 1)]
    p.wait("gpsimd", *[sdone[(npair - 1, tt)] for tt in range(nTt - 4, nTt)])


def run_l3attn(q_full, ckv_full, wukT_bf, wuvT_bf, slope=0.0625):
    nc = get_prog("l3attn", b_l3attn)
    nblkS = SEQ // 128
    alib = np.zeros((128, nblkS), np.float32)
    for b in range(nblkS):
        alib[:, b] = slope * (b * 128 + np.arange(128))
    srow = (slope * np.arange(SEQ, dtype=np.float32))[None, :]
    mm = np.zeros((128, 4 * 512), np.float32)
    for r in range(4):
        tk = r * 128 + np.arange(128)[:, None]
        tq = np.arange(512)[None, :]
        mm[:, r * 512:(r + 1) * 512] = (tk <= tq)
    maps = []
    for c in range(NC):
        bt = c // 4
        cols = slice(bt * SEQ, (bt + 1) * SEQ)
        m = {"ckv": np.ascontiguousarray(ckv_full[:, cols]),
             "alib": alib, "srow": srow, "masks": to_bf(mm),
             "hsel": np.zeros((1, 4), np.int32)}
        for i in range(4):
            pair = 4 * c + i
            h = pair % 16
            rows = slice(128 * h, 128 * h + 128)
            m[f"q{i}"] = np.ascontiguousarray(q_full[rows, cols])
            m[f"wuk{i}"] = np.ascontiguousarray(wukT_bf[:, 128 * h:128 * h + 128])
            m[f"wuv{i}"] = np.ascontiguousarray(wuvT_bf[:, 128 * h:128 * h + 128])
        maps.append(m)
    res = launch(nc, maps)
    y_full = np.zeros((2048, NTOK), BF)
    for c in range(NC):
        bt = c // 4
        for i in range(4):
            pair = 4 * c + i
            h = pair % 16
            y_full[128 * h:128 * h + 128, bt * SEQ:(bt + 1) * SEQ] = \
                np.asarray(res[c][f"y{i}"])
    return y_full



def b_abssum(p, F_):
    w = p.inp("w", [128, F_], F32)
    o = p.outp("o", [1, 1], F32)
    tile = 16384
    nt = (F_ + tile - 1) // tile
    wt_ = [p.sb(f"w{j}", [128, tile]) for j in range(2)]
    acc = p.sb("acc", [128, 2], F32)
    row = p.sb("row", [1, 1], F32)
    ones = p.sb("ones", [128, 1], F32)
    psr = p.ps("psr", [1, 1])
    p.op("vector", lambda E: E.memset(acc[:], 0.0))
    mo = p.op("vector", lambda E: E.memset(ones[:], 1.0), mark=True)
    for i in range(nt):
        j = i % 2
        lo = i * tile
        wd = min(tile, F_ - lo)
        t = p.dma("sync", wt_[j][:, 0:wd], w[:, lo:lo + wd])
        p.wait("vector", t)
        rt = p.op("vector", lambda E, j=j, wd=wd: E.tensor_reduce(
            out=acc[:, 1:2], in_=wt_[j][:, 0:wd], axis=AX.X, op=ALU.add,
            apply_absolute_value=True), mark=True)
        p.wait("vector", rt)
        at = p.op("vector", lambda E: E.tensor_tensor(
            out=acc[:, 0:1], in0=acc[:, 0:1], in1=acc[:, 1:2], op=ALU.add),
            mark=True)
        p.wait("vector", at)
    p.wait("tensor", at, mo)
    mt = p.op("tensor", lambda E: E.matmul(
        psr[:, :], ones[:, :], acc[:, 0:1], start=True, stop=True), mark=True)
    p.wait("vector", mt)
    vt = p.op("vector", lambda E: E.tensor_copy(out=row[:], in_=psr[:, :]),
              mark=True)
    p.wait("vector", vt)
    p.wait("sync", vt)
    st = p.dma("sync", o[:], row[:])
    p.wait("gpsimd", st)


def b_l3tern(p, F_, nelem):
    w = p.inp("w", [128, F_], F32)
    parts = p.inp("parts", [8, 1], F32)
    o = p.outp("o", [128, F_], BF16)
    tile = 8192
    nt = (F_ + tile - 1) // tile
    wt_ = [p.sb(f"w{j}", [128, tile]) for j in range(2)]
    m1 = p.sb("m1", [128, tile])
    m2 = p.sb("m2", [128, tile])
    stg = [p.sb(f"s{j}", [128, tile], BF16) for j in range(2)]
    pt = p.sb("pt", [8, 1], F32)
    row = p.sb("row", [1, 4], F32)
    cols = p.sb("cols", [128, 3], F32)
    ones8 = p.sb("ones8", [8, 1], F32)
    onesr = p.sb("onesr", [1, 128], F32)
    psr = p.ps("psr", [128, 4])

    t = p.dma("sync", pt[:], parts[:])
    mo = p.op("vector", lambda E: E.memset(ones8[:], 1.0), mark=True)
    mo2 = p.op("vector", lambda E: E.memset(onesr[:], 1.0), mark=True)
    p.wait("tensor", t, mo)
    st_ = p.op("tensor", lambda E: E.matmul(
        psr[0:1, 0:1], ones8[:, :], pt[:, :], start=True, stop=True),
        mark=True)
    p.wait("vector", st_)

    def vop(fn):
        tok = p.op("vector", fn, mark=True)
        p.wait("vector", tok)
        return tok
    vop(lambda E: E.tensor_scalar(
        out=row[:, 0:1], in0=psr[0:1, 0:1], scalar1=1.0 / nelem, scalar2=1e-5,
        op0=ALU.mult, op1=ALU.max))
    vop(lambda E: E.tensor_scalar(
        out=row[:, 1:2], in0=row[:, 0:1], scalar1=0.7, scalar2=None,
        op0=ALU.mult))
    rt = vop(lambda E: E.tensor_scalar(
        out=row[:, 2:3], in0=row[:, 1:2], scalar1=-1.0, scalar2=None,
        op0=ALU.mult))
    p.wait("tensor", rt, mo2)
    bt = p.op("tensor", lambda E: E.matmul(
        psr[:, 0:3], onesr[:, :], row[:, 0:3], start=True, stop=True),
        mark=True)
    p.wait("vector", bt)
    ct = vop(lambda E: E.tensor_copy(out=cols[:], in_=psr[:, 0:3]))
    sdone = [None, None]
    for i in range(nt):
        j = i % 2
        lo = i * tile
        wd = min(tile, F_ - lo)
        if sdone[j] is not None:
            p.wait("sync", sdone[j])
        t = p.dma("sync", wt_[j][:, 0:wd], w[:, lo:lo + wd])
        p.wait("vector", t)
        p.op("vector", lambda E, j=j, wd=wd: E.tensor_scalar(
            out=m1[:, 0:wd], in0=wt_[j][:, 0:wd], scalar1=cols[:, 1:2],
            scalar2=None, op0=ALU.is_gt))
        p.op("vector", lambda E, j=j, wd=wd: E.tensor_scalar(
            out=m2[:, 0:wd], in0=wt_[j][:, 0:wd], scalar1=cols[:, 2:3],
            scalar2=None, op0=ALU.is_lt))
        p.op("vector", lambda E, wd=wd: E.tensor_tensor(
            out=m1[:, 0:wd], in0=m1[:, 0:wd], in1=m2[:, 0:wd],
            op=ALU.subtract))
        if sdone[j] is not None:
            p.wait("vector", sdone[j])
        vt = p.op("vector", lambda E, j=j, wd=wd: E.tensor_scalar(
            out=stg[j][:, 0:wd], in0=m1[:, 0:wd], scalar1=cols[:, 0:1],
            scalar2=None, op0=ALU.mult), mark=True)
        p.wait("sync", vt)
        sdone[j] = p.dma("sync", o[:, lo:lo + wd], stg[j][:, 0:wd])
    p.wait("gpsimd", sdone[0], sdone[1])


def host_l3tern(WT_tiled, nelem):
    nM, _, Kd = WT_tiled.shape
    per = -(-nM // NC)
    F_ = per * Kd
    shards = []
    for c in range(NC):
        sl = list(range(c * per, min((c + 1) * per, nM)))
        wsl = np.zeros((per, 128, Kd), np.float32)
        wsl[:len(sl)] = WT_tiled[sl]
        shards.append(np.ascontiguousarray(
            wsl.transpose(1, 0, 2).reshape(128, F_)))
    nc1 = get_prog(f"abssum_{F_}", b_abssum, F_)
    res = launch(nc1, [{"w": s} for s in shards])
    parts = np.concatenate([np.asarray(res[c]["o"]) for c in range(NC)],
                           axis=0)
    nc2 = get_prog(f"l3tern_{F_}_{nelem}", b_l3tern, F_, nelem)
    res = launch(nc2, [{"w": shards[c], "parts": parts} for c in range(NC)])
    out = np.zeros((NC * per, 128, Kd), BF)
    for c in range(NC):
        out[c * per:(c + 1) * per] = np.asarray(
            res[c]["o"]).reshape(128, per, Kd).transpose(1, 0, 2)
    return np.ascontiguousarray(out[:nM])


def layer3(I, X, WB):
    XN_f, XN_b = run_ln("top3", X, eps=1e-5)
    CKV = run_proj("ckv", XN_b, WB["l3_wdkv"], out_np=BF)
    Q = run_proj("q3", XN_b, WB["l3_wq"], out_np=BF)
    ckv_full = unshard_cols(CKV)
    q_full = unshard_cols(Q)
    y_full = run_l3attn(q_full, ckv_full, WB["l3_wukT_nat"],
                        WB["l3_wuvT_nat"])
    Y = shard_cols(y_full)
    S = run_proj("s3", Y, WB["l3_wo"])
    H = run_proj("h3", XN_b, WB["l3_wfc"], epilogue="relusq", out_np=BF)
    G = run_proj("g3", H, WB["l3_wproj"], epilogue="silu", ksplit=2)
    return run_ew("combine", S, G, X=X)



def b_l2conv(p, Tn=T, CH=128):
    nX = 4352 // 128
    xBC = p.inp("xBC", [4352, Tn], F32)
    bnd = p.inp("bnd", [4352, 4], F32)
    cw = p.inp("cw", [128, 4 * (4352 // 128)], F32)
    dt_in = p.inp("dt", [64, Tn], F32)
    dtb = p.inp("dtb", [64, 1], F32)
    alog = p.inp("alog", [64, 1], F32)
    x_bf = p.outp("x_bf", [4096, Tn], BF16)
    B_bf = p.outp("B_bf", [128, Tn], BF16)
    C_bf = p.outp("C_bf", [128, Tn], BF16)
    dtp_o = p.outp("dtp", [64, Tn], F32)
    acs_o = p.outp("acs", [64, Tn], F32)

    ext = [p.sb(f"e{j}", [128, Tn + 3]) for j in range(2)]
    cwt = p.sb("cwt", [128, 4 * nX], F32)
    acc = p.sb("acc", [128, Tn], F32)
    acc2 = p.sb("acc2", [128, Tn], F32)
    stg = [p.sb(f"s{j}", [128, Tn], BF16) for j in range(2)]
    dtt = p.sb("dtt", [64, Tn], F32)
    dtc = p.sb("dtc", [64, 3], F32)
    t1 = p.sb("t1", [64, Tn], F32)
    t2 = p.sb("t2", [64, Tn], F32)
    zz = p.sb("zz", [64, CH], F32)
    onec = p.sb("onec", [64, 1], F32)

    cwtok = p.dma("sync", cwt[:], cw[:])
    p.wait("vector", cwtok)
    sdone = [None, None]
    for ci in range(nX):
        j = ci % 2
        if sdone[j] is not None:
            p.wait("sync", sdone[j])
        tb = p.dma("sync", ext[j][:, 0:3], bnd[ci * 128:(ci + 1) * 128, 0:3])
        tm = p.dma("sync", ext[j][:, 3:], xBC[ci * 128:(ci + 1) * 128, :])
        p.wait("vector", tb, tm)
        c0 = ci * 4
        p.op("vector", lambda E, j=j, c0=c0: E.tensor_scalar(
            out=acc[:], in0=ext[j][:, 0:Tn], scalar1=cwt[:, c0:c0 + 1],
            scalar2=None, op0=ALU.mult))
        p.op("vector", lambda E, j=j, c0=c0: E.scalar_tensor_tensor(
            out=acc[:], in0=ext[j][:, 1:Tn + 1], scalar=cwt[:, c0 + 1:c0 + 2],
            in1=acc[:], op0=ALU.mult, op1=ALU.add))
        p.op("vector", lambda E, j=j, c0=c0: E.scalar_tensor_tensor(
            out=acc[:], in0=ext[j][:, 2:Tn + 2], scalar=cwt[:, c0 + 2:c0 + 3],
            in1=acc[:], op0=ALU.mult, op1=ALU.add))
        vt = p.op("vector", lambda E, j=j, c0=c0: E.scalar_tensor_tensor(
            out=acc[:], in0=ext[j][:, 3:Tn + 3], scalar=cwt[:, c0 + 3:c0 + 4],
            in1=acc[:], op0=ALU.mult, op1=ALU.add), mark=True)
        p.wait("scalar", vt)
        if sdone[j] is not None:
            p.wait("scalar", sdone[j])
        st = p.op("scalar", lambda E, j=j: E.activation(
            out=stg[j][:], in_=acc[:], func=AF.Silu), mark=True)
        p.wait("sync", st)
        if ci < 32:
            dst = x_bf[ci * 128:(ci + 1) * 128, :]
        elif ci == 32:
            dst = B_bf[:, :]
        else:
            dst = C_bf[:, :]
        sdone[j] = p.dma("sync", dst, stg[j][:])
    dtok = p.dma("sync", dtt[:], dt_in[:])
    btok = p.dma("sync", dtc[:, 0:1], dtb[:])
    atok = p.dma("sync", dtc[:, 1:2], alog[:])
    p.wait("vector", dtok, btok, atok)

    def vop(fn):
        tok = p.op("vector", fn, mark=True)
        p.wait("vector", tok)
        return tok
    mo = p.op("vector", lambda E: E.memset(onec[:], 1.0), mark=True)
    mz = p.op("vector", lambda E: E.memset(zz[:], 0.0), mark=True)
    xb = vop(lambda E: E.tensor_scalar(
        out=dtt[:], in0=dtt[:], scalar1=dtc[:, 0:1], scalar2=None,
        op0=ALU.add))
    p.wait("scalar", xb, mo)
    ab = p.op("scalar", lambda E: E.activation(
        out=t1[:], in_=dtt[:], func=AF.Abs), mark=True)
    eab = p.op("scalar", lambda E: E.activation(
        out=t1[:], in_=t1[:], func=AF.Exp, scale=-1.0), mark=True)
    p.wait("scalar", eab)
    lab = p.op("scalar", lambda E: E.activation(
        out=t1[:], in_=t1[:], func=AF.Ln, bias=onec[:, 0:1]), mark=True)
    p.wait("vector", lab)
    vop(lambda E: E.tensor_scalar(
        out=t2[:], in0=dtt[:], scalar1=0.0, scalar2=None, op0=ALU.max))
    dtpt = vop(lambda E: E.tensor_tensor(
        out=dtt[:], in0=t2[:], in1=t1[:], op=ALU.add))
    p.wait("scalar", dtpt)
    ae = p.op("scalar", lambda E: E.activation(
        out=dtc[:, 2:3], in_=dtc[:, 1:2], func=AF.Exp), mark=True)
    p.wait("vector", ae)
    na = vop(lambda E: E.tensor_scalar(
        out=dtc[:, 2:3], in0=dtc[:, 2:3], scalar1=-1.0, scalar2=None,
        op0=ALU.mult))
    vop(lambda E: E.tensor_scalar(
        out=t1[:], in0=dtt[:], scalar1=dtc[:, 2:3], scalar2=None,
        op0=ALU.mult))
    p.wait("vector", mz)
    cstoks = []
    for c in range(Tn // CH):
        sl = slice(c * CH, (c + 1) * CH)
        cstoks.append(p.op("vector", lambda E, sl=sl: E.tensor_tensor_scan(
            out=t2[:, sl], data0=zz[:, :], data1=t1[:, sl], initial=0.0,
            op0=ALU.add, op1=ALU.add), mark=True))
    p.wait("sync", dtpt, *cstoks)
    st1 = p.dma("sync", dtp_o[:], dtt[:])
    st2 = p.dma("sync", acs_o[:], t2[:])
    p.wait("gpsimd", st1, st2, sdone[0], sdone[1])



def b_l2chunkA(p, Tn=T, CH=128):
    nCk = Tn // CH
    x_bf = p.inp("x_bf", [4096, Tn], BF16)
    B_bf = p.inp("B_bf", [128, Tn], BF16)
    dtp = p.inp("dtp", [64, Tn], F32)
    acs = p.inp("acs", [64, Tn], F32)
    idb = p.inp("idb", [128, 128], BF16)
    states = p.outp("states", [nCk, 128, 4096], F32)
    cdec = p.outp("cdec", [64, nCk], F32)

    acst = p.sb("acst", [64, Tn], F32)
    dtpt = p.sb("dtpt", [64, Tn], F32)
    dte = p.sb("dte", [64, CH], F32)
    wmix = p.sb("wmix", [64, CH], BF16)
    wtm = p.sb("wtm", [128, 64], BF16)
    xtm = [p.sb(f"xt{j}", [128, 4096], BF16) for j in range(2)]
    btm = [p.sb(f"bt{j}", [128, 128], BF16) for j in range(2)]
    xdt = p.sb("xdt", [128, 4096], BF16)
    idt = p.sb("idt", [128, 128], BF16)
    cd = p.sb("cd", [64, nCk], F32)
    stg = [p.sb(f"sg{j}", [128, 512], F32) for j in range(4)]
    psT = p.ctx.enter_context(p.nc.psum_tensor("psT", [128, 64], BF16))
    psS = [p.ps(f"psS{i}", [128, 512]) for i in range(4)]

    t0 = p.dma("sync", acst[:], acs[:])
    t1_ = p.dma("sync", dtpt[:], dtp[:])
    t2_ = p.dma("sync", idt[:], idb[:])
    p.wait("scalar", t0)
    p.wait("vector", t0, t1_)
    prev_tp = None
    prev_wt = None
    prev_wtm_cons = None
    sdone = {}
    evtoks = {}
    xtok = [None, None]
    btok = [None, None]
    for c in range(nCk):
        j = c % 2
        sl = slice(c * CH, (c + 1) * CH)
        if c >= 2:
            p.wait("sync", sdone[(c - 2, 7)])
        xtok[j] = p.dma("sync", xtm[j][:], x_bf[:, sl], transpose=True)
        btok[j] = p.dma("sync", btm[j][:], B_bf[:, sl], transpose=True)
        if prev_wt is not None:
            p.wait("scalar", prev_wt)
        et = p.op("scalar", lambda E, sl=sl, c=c: E.activation(
            out=dte[:], in_=acst[:, sl], func=AF.Exp, scale=-1.0,
            bias=acst[:, c * CH + CH - 1:c * CH + CH]), mark=True)
        cdt = p.op("scalar", lambda E, c=c: E.activation(
            out=cd[:, c:c + 1], in_=acst[:, c * CH + CH - 1:c * CH + CH],
            func=AF.Exp), mark=True)
        p.wait("vector", et)
        if prev_wtm_cons is not None:
            p.wait("vector", prev_wtm_cons)
        wt = p.op("vector", lambda E, sl=sl: E.tensor_tensor(
            out=wmix[:], in0=dtpt[:, sl], in1=dte[:], op=ALU.mult), mark=True)
        prev_wt = wt
        p.wait("tensor", wt, t2_)
        if prev_tp is not None:
            p.wait("tensor", prev_tp)
        tp = p.op("tensor", lambda E: E.transpose(
            psT[:, :], wmix[:], idt[0:64, 0:64]), mark=True)
        p.wait("vector", tp)
        wev = p.op("vector", lambda E: E.tensor_copy(
            out=wtm[:], in_=psT[:, :]), mark=True)
        prev_tp = wev
        p.wait("vector", wev, xtok[j])
        if c > 0:
            p.wait("vector", evtoks[(c - 1, 7)])
        wrep = wtm[:].unsqueeze(2).broadcast_to([128, 64, 64])
        xdt3 = xdt[:].rearrange("p (h q) -> p h q", h=64)
        xm3 = xtm[j][:].rearrange("p (h q) -> p h q", h=64)
        xdtok = p.op("vector", lambda E, j=j, wrep=wrep, xdt3=xdt3, xm3=xm3:
                     E.tensor_tensor(out=xdt3, in0=xm3, in1=wrep,
                                     op=ALU.mult), mark=True)
        prev_wtm_cons = xdtok
        p.wait("tensor", xdtok, btok[j])
        for nt in range(8):
            idx = c * 8 + nt
            pp = psS[nt % 4]
            if idx >= 4:
                pidx = idx - 4
                p.wait("tensor", evtoks[(pidx // 8, pidx % 8)])
            mm = p.op("tensor", lambda E, j=j, nt=nt, pp=pp: E.matmul(
                pp[:, :], btm[j][:, :], xdt[:, nt * 512:(nt + 1) * 512],
                start=True, stop=True), mark=True)
            p.wait("vector", mm)
            if idx >= 4:
                pidx = idx - 4
                p.wait("vector", sdone[(pidx // 8, pidx % 8)])
            ev = p.op("vector", lambda E, nt=nt, pp=pp: E.tensor_copy(
                out=stg[nt % 4][:], in_=pp[:, :]), mark=True)
            evtoks[(c, nt)] = ev
            p.wait("sync", ev)
            sdone[(c, nt)] = p.dma(
                "sync", states[c, :, nt * 512:(nt + 1) * 512],
                stg[nt % 4][:])
    p.wait("sync", cdt)
    sl_ = p.dma("sync", cdec[:], cd[:])
    p.wait("gpsimd", sl_, *[sdone[(nCk - 1, nt)] for nt in range(8)])



def b_l2scan(p, nCkG=128, W=512, reset_at=64):
    sts = p.inp("sts", [nCkG, 128, W], F32)
    dec = p.inp("dec", [1, nCkG * 8], F32)
    prev = p.outp("prev", [nCkG, 128, W], BF16)
    h = p.sb("h", [128, W], F32)
    tmp = p.sb("tmp", [128, W], F32)
    sc = [p.sb(f"sc{j}", [128, W], F32) for j in range(2)]
    db = p.sb("db", [128, 8], F32)
    decr = p.sb("decr", [1, nCkG * 8], F32)
    stg = [p.sb(f"st{j}", [128, W], BF16) for j in range(2)]
    onesr = p.sb("onesr", [1, 128], F32)
    psD = p.ps("psD", [128, 8])

    dtok = p.dma("sync", decr[:], dec[:])
    mo = p.op("vector", lambda E: E.memset(onesr[:], 1.0), mark=True)
    stok = [None, None]
    ltok = [None, None]
    upd = [None, None]
    prev_db = None
    prev_mult = None
    for c in range(nCkG):
        j = c % 2
        if upd[j] is not None:
            p.wait("sync", upd[j])
        ltok[j] = p.dma("sync", sc[j][:], sts[c, :, :])
        if c % reset_at == 0:
            mres = p.op("vector", lambda E: E.memset(h[:], 0.0), mark=True)
            p.wait("vector", mres)
        p.wait("tensor", mo, dtok)
        if prev_db is not None:
            p.wait("tensor", prev_db)
        dmt = p.op("tensor", lambda E, c=c: E.matmul(
            psD[:, :], onesr[:, :], decr[:, c * 8:(c + 1) * 8],
            start=True, stop=True), mark=True)
        p.wait("scalar", dmt)
        if prev_mult is not None:
            p.wait("scalar", prev_mult)
        dbt = p.op("scalar", lambda E: E.copy(out=db[:], in_=psD[:, :]),
                   mark=True)
        prev_db = dbt
        if stok[j] is not None:
            p.wait("vector", stok[j])
        pv = p.op("vector", lambda E, j=j: E.tensor_copy(
            out=stg[j][:], in_=h[:]), mark=True)
        p.wait("sync", pv)
        stok[j] = p.dma("sync", prev[c, :, :], stg[j][:])
        p.wait("vector", dbt, ltok[j])
        dbrep = db[:].unsqueeze(2).broadcast_to([128, 8, 64])
        h3 = h[:].rearrange("p (a b) -> p a b", a=8)
        t3 = tmp[:].rearrange("p (a b) -> p a b", a=8)
        prev_mult = p.op("vector", lambda E, dbrep=dbrep, h3=h3, t3=t3:
                         E.tensor_tensor(out=t3, in0=h3, in1=dbrep,
                                         op=ALU.mult), mark=True)
        upd[j] = p.op("vector", lambda E, j=j: E.tensor_tensor(
            out=h[:], in0=tmp[:], in1=sc[j][:], op=ALU.add), mark=True)
    p.wait("gpsimd", stok[0], stok[1])



def b_l2chunkB(p, Tn=T, CH=128):
    nCk = Tn // CH
    x_bf = p.inp("x_bf", [4096, Tn], BF16)
    B_bf = p.inp("B_bf", [128, Tn], BF16)
    C_bf = p.inp("C_bf", [128, Tn], BF16)
    dtp = p.inp("dtp", [64, Tn], F32)
    acs = p.inp("acs", [64, Tn], F32)
    prev = p.inp("prev", [nCk, 128, 4096], BF16)
    idb = p.inp("idb", [128, 128], BF16)
    idf = p.inp("idf", [128, 128], F32)
    delta = p.inp("delta", [64, 16 * 512], F32)
    mask4 = p.inp("mask4", [128, 512], BF16)
    Drep = p.inp("Drep", [1, 4096], F32)
    Ytm = p.outp("Ytm", [Tn, 4096], BF16)

    Bfm = p.sb("Bfm", [128, Tn], BF16)
    Cfm = p.sb("Cfm", [128, Tn], BF16)
    acst = p.sb("acst", [64, Tn], F32)
    dtpt = p.sb("dtpt", [64, Tn], F32)
    idtb = p.sb("idtb", [128, 128], BF16)
    idtf = p.sb("idtf", [128, 128], F32)
    delt = p.sb("delt", [64, 16 * 512], F32)
    mskt = p.sb("mskt", [128, 512], BF16)
    drt = p.sb("drt", [1, 4096], F32)
    Dbc = p.sb("Dbc", [128, 4096], F32)
    ones64 = p.sb("ones64", [64, 128], F32)
    onesr = p.sb("onesr", [1, 128], F32)
    xtm = [p.sb(f"xt{j}", [128, 4096], BF16) for j in range(2)]
    pvt = [p.sb(f"pv{j}", [128, 4096], BF16) for j in range(2)]
    na = p.sb("na", [64, CH], F32)
    dcb = p.sb("dcb", [64, CH], BF16)
    wdtm = p.sb("wdtm", [128, 64], BF16)
    acstm = p.sb("acstm", [128, 64], F32)
    expT = p.sb("expT", [128, 64], F32)
    xdt = p.sb("xdt", [128, 4096], BF16)
    scT = p.sb("scT", [128, CH], BF16)
    tmp512 = p.sb("tmp512", [64, 512], F32)
    Lb = p.sb("Lb", [128, 512], BF16)
    P4 = p.sb("P4", [128, 512], BF16)
    yo = p.sb("yo", [128, 256], F32)
    y4 = p.sb("y4", [128, 256], F32)
    dx = p.sb("dx", [128, 256], F32)
    stg = [p.sb(f"sg{j}", [128, 256], BF16) for j in range(2)]
    psTb = p.ctx.enter_context(p.nc.psum_tensor("psTb", [128, 64], BF16))
    psTf = p.ps("psTf", [128, 64])
    psS2 = p.ps("psS2", [128, CH])
    psSeg = [p.ps(f"psSeg{i}", [128, 512]) for i in range(2)]
    psY = p.ps("psY", [128, 256])
    psY2 = p.ps("psY2", [128, 256])

    t0 = p.dma("sync", Bfm[:], B_bf[:])
    t1_ = p.dma("sync", Cfm[:], C_bf[:])
    t2_ = p.dma("sync", acst[:], acs[:])
    t3_ = p.dma("sync", dtpt[:], dtp[:])
    t4_ = p.dma("sync", idtb[:], idb[:])
    t5_ = p.dma("sync", idtf[:], idf[:])
    t6_ = p.dma("sync", delt[:], delta[:])
    t7_ = p.dma("sync", mskt[:], mask4[:])
    t8_ = p.dma("sync", drt[:], Drep[:])
    mo = p.op("vector", lambda E: E.memset(ones64[:], 1.0), mark=True)
    mo2 = p.op("vector", lambda E: E.memset(onesr[:], 1.0), mark=True)
    p.wait("tensor", t8_, mo2)
    dbl = None
    for ot in range(8):
        dmt = p.op("tensor", lambda E, ot=ot: E.matmul(
            psSeg[ot % 2][:, :], onesr[:, :], drt[:, ot * 512:(ot + 1) * 512],
            start=True, stop=True), mark=True)
        p.wait("scalar", dmt)
        dbl = p.op("scalar", lambda E, ot=ot: E.copy(
            out=Dbc[:, ot * 512:(ot + 1) * 512], in_=psSeg[ot % 2][:, :]),
            mark=True)
        p.wait("tensor", dbl)
    p.wait("vector", t2_, t3_, dbl)
    p.wait("scalar", t2_)

    sdone = {}
    evtoks = {}
    xtok = [None, None]
    ptok = [None, None]
    prev_psY = None
    prev_sg2 = None
    prev_p4t = None
    prev_ydl = None
    prev_yotok = None
    prev_trans = None
    prev_scT = None
    prev_segE = {}
    prev_P4cons = None
    prev_yev = None
    for c in range(nCk):
        j = c % 2
        sl = slice(c * CH, (c + 1) * CH)
        if c >= 2:
            p.wait("sync", sdone[(c - 2, 15)])
        xtok[j] = p.dma("sync", xtm[j][:], x_bf[:, sl], transpose=True)
        ptok[j] = p.dma("sync", pvt[j][:], prev[c, :, :])
        if prev_sg2 is not None:
            p.wait("vector", prev_sg2)
        nat = p.op("vector", lambda E, sl=sl: E.tensor_scalar(
            out=na[:], in0=acst[:, sl], scalar1=-1.0, scalar2=None,
            op0=ALU.mult), mark=True)
        dct = p.op("vector", lambda E, sl=sl: E.tensor_copy(
            out=dcb[:], in_=dtpt[:, sl]), mark=True)
        p.wait("tensor", dct, t4_, t5_)
        if prev_trans is not None:
            p.wait("tensor", prev_trans)
        tp1 = p.op("tensor", lambda E: E.transpose(
            psTb[:, :], dcb[:], idtb[0:64, 0:64]), mark=True)
        tp2 = p.op("tensor", lambda E, sl=sl: E.transpose(
            psTf[:, :], acst[:, sl], idtf[0:64, 0:64]), mark=True)
        p.wait("vector", tp1, tp2)
        wev = p.op("vector", lambda E: E.tensor_copy(
            out=wdtm[:], in_=psTb[:, :]), mark=True)
        aev = p.op("vector", lambda E: E.tensor_copy(
            out=acstm[:], in_=psTf[:, :]), mark=True)
        prev_trans = aev
        p.wait("scalar", aev)
        if prev_yotok is not None:
            p.wait("scalar", prev_yotok)
        eT = p.op("scalar", lambda E: E.activation(
            out=expT[:], in_=acstm[:], func=AF.Exp), mark=True)
        p.wait("vector", wev, xtok[j], eT)
        if prev_ydl is not None:
            p.wait("vector", prev_ydl)
        wrep = wdtm[:].unsqueeze(2).broadcast_to([128, 64, 64])
        xdt3 = xdt[:].rearrange("p (h q) -> p h q", h=64)
        xm3 = xtm[j][:].rearrange("p (h q) -> p h q", h=64)
        xdtok = p.op("vector", lambda E, wrep=wrep, xdt3=xdt3, xm3=xm3:
                     E.tensor_tensor(out=xdt3, in0=xm3, in1=wrep,
                                     op=ALU.mult), mark=True)
        p.wait("tensor", t0, t1_)
        if prev_scT is not None:
            p.wait("tensor", prev_scT)
        smm = p.op("tensor", lambda E, sl=sl: E.matmul(
            psS2[:, :], Bfm[:, sl], Cfm[:, sl], start=True, stop=True),
            mark=True)
        p.wait("scalar", smm)
        if prev_p4t is not None:
            p.wait("scalar", prev_p4t)
        scev = p.op("scalar", lambda E: E.copy(out=scT[:], in_=psS2[:, :]),
                    mark=True)
        prev_scT = scev
        p.wait("tensor", xdtok, ptok[j], mo, nat)
        p.wait("vector", scev, t6_, t7_)
        for hg in range(16):
            jj = hg % 2
            dts = delt[:, hg * 512:(hg + 1) * 512]
            acr = acst[:, sl].unsqueeze(1).broadcast_to([64, 4, CH])
            dt3 = dts.rearrange("p (a b) -> p a b", a=4)
            tm3 = tmp512[:].rearrange("p (a b) -> p a b", a=4)
            if prev_sg2 is not None:
                p.wait("vector", prev_sg2)
            tmt = p.op("vector", lambda E, dt3=dt3, acr=acr, tm3=tm3:
                       E.tensor_tensor(out=tm3, in0=dt3, in1=acr,
                                       op=ALU.mult), mark=True)
            segk = (c, hg - 2)
            if segk in prev_segE:
                p.wait("tensor", prev_segE[segk])
            elif hg < 2 and prev_segE.get("last") is not None:
                p.wait("tensor", prev_segE["last"])
            p.op("tensor", lambda E, jj=jj, dts=dts: E.matmul(
                psSeg[jj][:, :], na[:], dts, start=True, stop=False))
            p.wait("tensor", tmt)
            sg2 = p.op("tensor", lambda E, jj=jj: E.matmul(
                psSeg[jj][:, :], ones64[:], tmp512[:], start=False,
                stop=True), mark=True)
            prev_sg2 = sg2
            p.wait("vector", sg2)
            clt = p.op("vector", lambda E, jj=jj: E.tensor_scalar(
                out=psSeg[jj][:, :], in0=psSeg[jj][:, :], scalar1=0.0,
                scalar2=None, op0=ALU.min), mark=True)
            p.wait("scalar", clt)
            if prev_P4cons is not None:
                p.wait("scalar", prev_P4cons)
            eL = p.op("scalar", lambda E, jj=jj: E.activation(
                out=Lb[:], in_=psSeg[jj][:, :], func=AF.Exp), mark=True)
            prev_segE[(c, hg)] = eL
            prev_segE["last"] = eL
            p.wait("vector", eL)
            p.op("vector", lambda E: E.tensor_tensor(
                out=Lb[:], in0=Lb[:], in1=mskt[:], op=ALU.mult))
            screp = scT[:].unsqueeze(1).broadcast_to([128, 4, CH])
            L3v = Lb[:].rearrange("p (a b) -> p a b", a=4)
            P3v = P4[:].rearrange("p (a b) -> p a b", a=4)
            p4t = p.op("vector", lambda E, L3v=L3v, screp=screp, P3v=P3v:
                       E.tensor_tensor(out=P3v, in0=L3v, in1=screp,
                                       op=ALU.mult), mark=True)
            prev_p4t = p4t
            p.wait("tensor", p4t)
            if prev_psY is not None:
                p.wait("tensor", prev_psY[0], prev_psY[1])
            ydl = yol = None
            for hh in range(4):
                h_ = hg * 4 + hh
                ydl = p.op("tensor", lambda E, hh=hh, h_=h_: E.matmul(
                    psY[:, hh * 64:(hh + 1) * 64],
                    P4[:, hh * CH:(hh + 1) * CH],
                    xdt[:, h_ * 64:(h_ + 1) * 64],
                    start=True, stop=True), mark=(hh == 3))
                yol = p.op("tensor", lambda E, hh=hh, h_=h_, j=j, sl=sl:
                           E.matmul(
                    psY2[:, hh * 64:(hh + 1) * 64], Cfm[:, sl],
                    pvt[j][:, h_ * 64:(h_ + 1) * 64],
                    start=True, stop=True), mark=(hh == 3))
            prev_P4cons = ydl
            prev_ydl = ydl
            p.wait("vector", ydl, yol)
            exslice = expT[:, hg * 4:hg * 4 + 4]
            exrep = exslice.unsqueeze(2).broadcast_to([128, 4, 64])
            yo3 = yo[:].rearrange("p (a b) -> p a b", a=4)
            pv3 = psY2[:, :].rearrange("p (a b) -> p a b", a=4)
            yotok = p.op("vector", lambda E, exrep=exrep, yo3=yo3, pv3=pv3:
                         E.tensor_tensor(out=yo3, in0=pv3, in1=exrep,
                                         op=ALU.mult), mark=True)
            prev_yotok = yotok
            yev1 = p.op("vector", lambda E: E.tensor_tensor(
                out=y4[:], in0=psY[:, :], in1=yo[:], op=ALU.add), mark=True)
            prev_psY = (yotok, yev1)
            p.op("vector", lambda E, j=j, hg=hg: E.tensor_tensor(
                out=dx[:], in0=xtm[j][:, hg * 256:(hg + 1) * 256],
                in1=Dbc[:, hg * 256:(hg + 1) * 256], op=ALU.mult))
            sk = (c * 16 + hg - 2)
            if sk >= 0:
                p.wait("vector", sdone[(sk // 16, sk % 16)])
            yfin = p.op("vector", lambda E, hg=hg: E.tensor_tensor(
                out=stg[hg % 2][:], in0=y4[:], in1=dx[:], op=ALU.add),
                mark=True)
            prev_yev = yfin
            p.wait("sync", yfin)
            sdone[(c, hg)] = p.dma(
                "sync", Ytm[sl, hg * 256:(hg + 1) * 256], stg[hg % 2][:])
    p.wait("gpsimd", *[sdone[(nCk - 1, hg)] for hg in range(16)])



def b_l2gate(p, eps=1e-5, Df=4096, Tn=T):
    nD = Df // 128
    nT = Tn // 512
    Ytm = p.inp("Ytm", [Tn, Df], BF16)
    z = p.inp("z", [Df, Tn], F32)
    nw = p.inp("nw", [128, nD], F32)
    out = p.outp("yn", [Df, Tn], BF16)

    yt = [p.sb(f"y{k}", [128, Tn], BF16) for k in range(nD)]
    zt = [p.sb(f"z{j}", [128, Tn], F32) for j in range(2)]
    zs = p.sb("zs", [128, Tn], F32)
    yg = p.sb("yg", [128, Tn], F32)
    sq = p.sb("sq", [128, 512], BF16)
    onesb = p.sb("onesb", [128, 1], BF16)
    onesr = p.sb("onesr", [1, 128], F32)
    rows = p.sb("rows", [1, 1024], F32)
    brd1 = p.sb("brd1", [128, Tn], F32)
    epsc = p.sb("epsc", [1, 1], F32)
    nwc = p.sb("nwc", [128, nD], F32)
    stg = [p.sb(f"s{j}", [128, Tn], BF16) for j in range(2)]
    ps2 = p.ps("ps2", [1, 512])
    psb = p.ps("psb", [128, 512])

    ytoks = [p.dma("sync", yt[k][:], Ytm[:, k * 128:(k + 1) * 128],
                   transpose=True) for k in range(nD)]
    nwt = p.dma("sync", nwc[:], nw[:])
    m0 = p.op("vector", lambda E: E.memset(epsc[:], float(eps)), mark=True)
    m2 = p.op("vector", lambda E: E.memset(onesb[:], 1.0), mark=True)
    m3 = p.op("vector", lambda E: E.memset(onesr[:], 1.0), mark=True)
    p.wait("scalar", m0)
    rowdone = None
    brddone = {}
    zdone = [None, None]
    sqdone = {}
    for tt in range(nT):
        s2tok = None
        for k in range(nD):
            j = k % 2
            if zdone[j] is not None:
                p.wait("sync", zdone[j])
            zk = p.dma("sync", zt[j][:, 0:512], z[k * 128:(k + 1) * 128,
                                                  tt * 512:(tt + 1) * 512])
            p.wait("scalar", zk)
            zst = p.op("scalar", lambda E, j=j: E.activation(
                out=zs[:, 0:512], in_=zt[j][:, 0:512], func=AF.Silu),
                mark=True)
            zdone[j] = zst
            p.wait("vector", zst, ytoks[k])
            ygt = p.op("vector", lambda E, k=k, tt=tt: E.tensor_tensor(
                out=yg[:, 0:512], in0=yt[k][:, tt * 512:(tt + 1) * 512],
                in1=zs[:, 0:512], op=ALU.mult), mark=True)
            p.wait("scalar", ygt)
            if k >= 1:
                p.wait("scalar", sqdone[k - 1])
            sqt = p.op("scalar", lambda E: E.activation(
                out=sq[:, :], in_=yg[:, 0:512], func=AF.Square), mark=True)
            p.wait("tensor", sqt, m2)
            s2tok = p.op("tensor", lambda E, k=k: E.matmul(
                ps2[:, :], onesb[:, :], sq[:, :], start=(k == 0),
                stop=(k == nD - 1)), mark=True)
            sqdone[k] = s2tok
            zdone[j] = ygt
        s2m = rows[:, 0:512]
        rstd = rows[:, 512:1024]
        p.wait("vector", s2tok)
        vtok = p.op("vector", lambda E, s2m=s2m: E.tensor_scalar(
            out=s2m, in0=ps2[:, :], scalar1=1.0 / Df, scalar2=None,
            op0=ALU.mult), mark=True)
        p.wait("scalar", vtok)
        p.op("scalar", lambda E, s2m=s2m, rstd=rstd: E.activation(
            out=rstd, in_=s2m, func=AF.Ln, bias=epsc[:, 0:1]))
        rtok = p.op("scalar", lambda E, rstd=rstd: E.activation(
            out=rstd, in_=rstd, func=AF.Exp, scale=-0.5), mark=True)
        rowdone = rtok
        p.wait("tensor", rtok, m3)
        if tt >= 1:
            p.wait("tensor", brddone[tt - 1])
        bt1 = p.op("tensor", lambda E, rstd=rstd: E.matmul(
            psb[:, :], onesr[:, :], rstd, start=True, stop=True), mark=True)
        p.wait("scalar", bt1)
        e1 = p.op("scalar", lambda E, tt=tt: E.copy(
            out=brd1[:, tt * 512:(tt + 1) * 512], in_=psb[:, :]), mark=True)
        brddone[tt] = e1
        p.wait("vector", e1)
        p.wait("tensor", vtok)
    sdone = [None, None]
    zdone2 = [None, None]
    for k in range(nD):
        j = k % 2
        if zdone2[j] is not None:
            p.wait("sync", zdone2[j])
        zk = p.dma("sync", zt[j][:], z[k * 128:(k + 1) * 128, :])
        p.wait("scalar", zk)
        if sdone[j] is not None:
            p.wait("scalar", sdone[j])
        zst = p.op("scalar", lambda E, j=j: E.activation(
            out=zs[:], in_=zt[j][:], func=AF.Silu), mark=True)
        p.wait("vector", zst)
        p.op("vector", lambda E, k=k, j=j: E.tensor_tensor(
            out=yg[:], in0=yt[k][:], in1=zs[:], op=ALU.mult))
        p.op("vector", lambda E: E.tensor_tensor(
            out=yg[:], in0=yg[:], in1=brd1[:], op=ALU.mult))
        p.wait("vector", nwt)
        vt = p.op("vector", lambda E, k=k, j=j: E.tensor_scalar(
            out=stg[j][:], in0=yg[:], scalar1=nwc[:, k:k + 1], scalar2=None,
            op0=ALU.mult), mark=True)
        zdone2[j] = vt
        p.wait("sync", vt)
        sdone[j] = p.dma("sync", out[k * 128:(k + 1) * 128, :], stg[j][:])
    p.wait("gpsimd", sdone[0], sdone[1])


def layer2(I, X, WB):
    XN_f, XN_b = run_ln("top2", X, eps=1e-5, out_f32=False, out_bf=True)
    ZXD = run_proj("inproj", XN_b, WB["l2_inproj"])
    Zs = [np.ascontiguousarray(z[0:4096]) for z in ZXD]
    xBC_full = unshard_cols([z[4096:8448] for z in ZXD])
    DTs = [np.ascontiguousarray(z[8448:8512]) for z in ZXD]
    nc1 = get_prog("l2conv", b_l2conv)
    maps = []
    for c in range(NC):
        bnd = np.zeros((4352, 4), np.float32)
        if c % 4 != 0:
            bnd[:, 0:3] = xBC_full[:, c * T - 3:c * T]
        maps.append({
            "xBC": np.ascontiguousarray(xBC_full[:, c * T:(c + 1) * T]),
            "bnd": bnd,
            "cw": np.ascontiguousarray(
                np.asarray(I["L2_attn_conv_w"]).reshape(34, 128, 4)
                .transpose(1, 0, 2).reshape(128, 136)),
            "dt": DTs[c],
            "dtb": np.ascontiguousarray(
                I["L2_attn_dt_bias"].astype(np.float32)[:, None]),
            "alog": np.ascontiguousarray(
                I["L2_attn_A_log"].astype(np.float32)[:, None]),
        })
    res = launch(nc1, maps)
    Xbf = [np.asarray(r["x_bf"]) for r in res]
    Bbf = [np.asarray(r["B_bf"]) for r in res]
    Cbf = [np.asarray(r["C_bf"]) for r in res]
    DTP = [np.asarray(r["dtp"]) for r in res]
    ACS = [np.asarray(r["acs"]) for r in res]
    idb = to_bf(np.eye(128, dtype=np.float32))
    idf = np.ascontiguousarray(np.eye(128, dtype=np.float32))
    nc2 = get_prog("l2chunkA", b_l2chunkA)
    res = launch(nc2, [{"x_bf": Xbf[c], "B_bf": Bbf[c], "dtp": DTP[c],
                        "acs": ACS[c], "idb": idb} for c in range(NC)])
    states_full = np.concatenate([np.asarray(r["states"]) for r in res], 0)
    cdec_full = np.concatenate([np.asarray(r["cdec"]) for r in res], 1)
    nc3 = get_prog("l2scan", b_l2scan)
    maps = []
    for c2 in range(NC):
        dec = np.zeros((1, 128 * 8), np.float32)
        for g in range(128):
            dec[0, g * 8:(g + 1) * 8] = cdec_full[8 * c2:8 * c2 + 8, g]
        maps.append({
            "sts": np.ascontiguousarray(
                states_full[:, :, 512 * c2:512 * (c2 + 1)]),
            "dec": dec,
        })
    res = launch(nc3, maps)
    prev_full = np.concatenate([np.asarray(r["prev"]) for r in res], 2)
    delta = np.zeros((64, 16 * 512), np.float32)
    for hg in range(16):
        for hh in range(4):
            delta[hg * 4 + hh, hg * 512 + hh * 128:(hg * 512 + (hh + 1) * 128)] = 1.0
    mask4 = np.zeros((128, 512), np.float32)
    ss = np.arange(128)[:, None]
    tq = np.arange(128)[None, :]
    for hh in range(4):
        mask4[:, hh * 128:(hh + 1) * 128] = (ss <= tq)
    Drep = np.ascontiguousarray(
        np.repeat(I["L2_attn_D_param"].astype(np.float32), 64)[None, :])
    nc4 = get_prog("l2chunkB", b_l2chunkB)
    res = launch(nc4, [{"x_bf": Xbf[c], "B_bf": Bbf[c], "C_bf": Cbf[c],
                        "dtp": DTP[c], "acs": ACS[c],
                        "prev": np.ascontiguousarray(
                            prev_full[16 * c:16 * (c + 1)]),
                        "idb": idb, "idf": idf, "delta": delta,
                        "mask4": to_bf(mask4), "Drep": Drep}
                       for c in range(NC)])
    Ytm = [np.asarray(r["Ytm"]) for r in res]
    nc5 = get_prog("l2gate", b_l2gate)
    nwcols = col_tile(I["L2_attn_norm_w"])
    res = launch(nc5, [{"Ytm": Ytm[c], "z": Zs[c], "nw": nwcols}
                       for c in range(NC)])
    YN = [np.asarray(r["yn"]) for r in res]
    S = run_proj("outproj", YN, WB["l2_outproj"])
    G = layer_fp4_ffn(XN_b, WB["l2_gu"], WB["l2_dn"])
    return run_ew("combine", S, G, X=X)


def layer0(I, X, WB):
    XN_f, XN_b = run_ln("top0", X, eps=1e-5)
    H1, _ = run_ln("h1", XN_f, eps=1e-5, w=I["L0_attn_ln1_w"],
                   b=I["L0_attn_ln1_b"], out_f32=True, out_bf=False)
    H1s = shard_cols(shift_cols(unshard_cols(H1)))
    AKb, AVb, ASRb = run_mix("l0m1", H1, H1s,
                             [I["L0_attn_tm_mix_k"], I["L0_attn_tm_mix_v"],
                              I["L0_attn_tm_mix_r"]], sig_last=True)
    Kf = run_proj("k", AKb, WB["l0_wk"])
    Vf = run_proj("v", AVb, WB["l0_wv"])
    Rf = run_proj("r", ASRb, WB["l0_wr"])
    wkv_full = run_scan(unshard_cols(Kf), unshard_cols(Vf),
                        np.asarray(I["L0_attn_tm_u"]),
                        np.asarray(I["L0_attn_tm_w"]))
    WKVRb = run_ew("mulbf", shard_cols(wkv_full), Rf)
    X1 = run_proj("att", WKVRb, WB["l0_wo"], epilogue="addR",
                  bias_cols=col_tile(I["L0_attn_tm_bo"]), R=XN_f)
    H2, _ = run_ln("h2", X1, eps=1e-5, w=I["L0_attn_ln2_w"],
                   b=I["L0_attn_ln2_b"], out_f32=True, out_bf=False)
    H2s = shard_cols(shift_cols(unshard_cols(H2)))
    BKb, BRb = run_mix("l0m2", H2, H2s,
                       [I["L0_attn_cm_mix_k"], I["L0_attn_cm_mix_r"]],
                       sig_last=False)
    KKb = run_proj("kk", BKb, WB["l0_cwk"], epilogue="relusq", out_np=BF)
    V2 = run_proj("v2", KKb, WB["l0_cwv"])
    R2 = run_proj("r2", BRb, WB["l0_cwr"], epilogue="sigmoid")
    S = run_ew("combine", R2, V2, X=X1)
    G = layer_fp4_ffn(XN_b, WB["l0_gu"], WB["l0_dn"])
    return run_ew("combine", S, G, X=X)



def b_fp4deq(p, I_, No):
    nI = I_ // 128
    wrep = p.inp("wrep", [I_, No], I32)
    sclb = p.inp("sclb", [I_, No], F16)
    shf = p.inp("shf", [128, 1], I32)
    out = p.outp("wt", [I_, No], BF16)

    wt_ = [p.sb(f"w{j}", [128, No], I32) for j in range(2)]
    sc16 = [p.sb(f"sc{j}", [128, No], F16) for j in range(2)]
    scf = p.sb("scf", [128, No], F32)
    t1 = p.sb("t1", [128, No], I32)
    m7 = p.sb("m7", [128, No], I32)
    hi = p.sb("hi", [128, No], I32)
    selT = p.sb("selT", [128, No], I32)
    selF = p.sb("selF", [128, No], I32)
    msk = p.sb("msk", [128, No], I32)
    bits = p.sb("bits", [128, No], I32)
    stg = [p.sb(f"s{j}", [128, No], BF16) for j in range(2)]
    shc = p.sb("shc", [128, 1], I32)

    stok = p.dma("sync", shc[:], shf[:])
    p.wait("vector", stok)
    wtok = [None, None]
    sctok = [None, None]
    sdone = [None, None]
    for ci in range(nI):
        j = ci % 2
        if sdone[j] is not None:
            p.wait("sync", sdone[j])
        wtok[j] = p.dma("sync", wt_[j][:], wrep[ci * 128:(ci + 1) * 128, :])
        sctok[j] = p.dma("sync", sc16[j][:], sclb[ci * 128:(ci + 1) * 128, :])
        p.wait("vector", wtok[j], sctok[j])

        def vop(fn):
            tok = p.op("vector", fn, mark=True)
            p.wait("vector", tok)
            return tok
        vop(lambda E, j=j: E.tensor_scalar(
            out=t1[:], in0=wt_[j][:], scalar1=shc[:, 0:1], scalar2=None,
            op0=ALU.logical_shift_right))
        vop(lambda E: E.tensor_scalar(
            out=m7[:], in0=t1[:], scalar1=7, scalar2=None,
            op0=ALU.bitwise_and))
        vop(lambda E: E.tensor_scalar(
            out=hi[:], in0=t1[:], scalar1=8, scalar2=28, op0=ALU.bitwise_and,
            op1=ALU.logical_shift_left))
        vop(lambda E: E.tensor_scalar(
            out=selT[:], in0=m7[:], scalar1=252, scalar2=None, op0=ALU.add))
        vop(lambda E: E.tensor_scalar(
            out=selT[:], in0=selT[:], scalar1=22, scalar2=None,
            op0=ALU.logical_shift_left))
        vop(lambda E: E.tensor_scalar(
            out=selF[:], in0=m7[:], scalar1=1056964608, scalar2=None,
            op0=ALU.mult))
        vop(lambda E: E.tensor_scalar(
            out=msk[:], in0=m7[:], scalar1=2, scalar2=None, op0=ALU.is_ge))
        vop(lambda E: E.select(
            out=bits[:], mask=msk[:], on_true=selT[:], on_false=selF[:]))
        vop(lambda E: E.tensor_tensor(
            out=t1[:], in0=bits[:], in1=hi[:], op=ALU.bitwise_or))
        vop(lambda E, j=j: E.tensor_copy(out=scf[:], in_=sc16[j][:]))
        if sdone[j] is not None:
            p.wait("vector", sdone[j])
        sfin = vop(lambda E, j=j: E.tensor_tensor(
            out=stg[j][:], in0=t1[:].bitcast(F32), in1=scf[:], op=ALU.mult))
        p.wait("sync", sfin)
        sdone[j] = p.dma("sync", out[ci * 128:(ci + 1) * 128, :], stg[j][:])
    p.wait("gpsimd", sdone[0], sdone[1])


def host_fp4_dequant(packed, scales, tag):
    O, IW = packed.shape
    I_ = IW * 8
    No = O // NC
    wrep_full = np.ascontiguousarray(np.repeat(packed.T, 8, axis=0))
    sclb_full = np.repeat(scales, 128, axis=0)
    shf = np.ascontiguousarray(
        np.tile((np.arange(8, dtype=np.int32) * 4), 16)[:, None])
    key = f"fp4deq_{I_}_{No}"
    nc = get_prog(key, b_fp4deq, I_, No)
    maps = []
    for c in range(NC):
        sl = slice(c * No, (c + 1) * No)
        maps.append({
            "wrep": np.ascontiguousarray(wrep_full[:, sl]),
            "sclb": np.ascontiguousarray(sclb_full[:, sl]),
            "shf": shf,
        })
    res = launch(nc, maps)
    WT = np.concatenate([np.asarray(res[c]["wt"]) for c in range(NC)], axis=1)
    t = WT.reshape(I_ // 128, 128, O // 128, 128)
    t = np.ascontiguousarray(t.transpose(2, 1, 0, 3))
    return t.reshape(O // 128, 128, I_)



def b_projpair(p, K, Mh, Tn=T):
    nK = K // 128
    nG = Mh // 128
    nT = Tn // 512
    A = p.inp("A", [K, Tn], BF16)
    W = p.inp("W", [2 * nG, 128, K], BF16)
    H = p.outp("H", [Mh, Tn], BF16)

    a_t = [p.sb(f"a{k}", [128, Tn], BF16) for k in range(nK)]
    wg_t = [p.sb(f"wg{j}", [128, K], BF16) for j in range(2)]
    wu_t = [p.sb(f"wu{j}", [128, K], BF16) for j in range(2)]
    sg = [p.sb(f"sg{j}", [128, 512], BF16) for j in range(2)]
    stg = [p.sb(f"s{j}", [128, Tn], BF16) for j in range(2)]
    psums = [p.ps(f"p{i}", [128, 512]) for i in range(4)]

    atoks = [p.dma("sync", a_t[k][:], A[k * 128:(k + 1) * 128, :])
             for k in range(nK)]
    wtok = [None, None]
    sfree = [None, None]
    wfree = [None, None]
    edone = {}
    for gi in range(nG):
        wb = gi % 2
        if wfree[wb] is not None:
            p.wait("gpsimd", wfree[wb])
        wtok[wb] = p.dma("gpsimd", wg_t[wb][:], W[gi, :, :])
        wtok2 = p.dma("gpsimd", wu_t[wb][:], W[nG + gi, :, :])
        p.wait("tensor", wtok[wb], wtok2)
        if gi == 0:
            p.wait("tensor", *atoks)
        for tt in range(nT):
            psg = psums[(2 * tt) % 4]
            psu = psums[(2 * tt + 1) % 4]
            idx = gi * nT + tt
            if idx >= 2:
                p.wait("tensor", edone[idx - 2])
            gm = um = None
            for ki in range(nK):
                gm = p.op("tensor", lambda E, psg=psg, wb=wb, ki=ki, tt=tt:
                          E.matmul(psg[:, :],
                                   wg_t[wb][:, ki * 128:(ki + 1) * 128],
                                   a_t[ki][:, tt * 512:(tt + 1) * 512],
                                   start=(ki == 0), stop=(ki == nK - 1)),
                          mark=(ki == nK - 1))
            for ki in range(nK):
                um = p.op("tensor", lambda E, psu=psu, wb=wb, ki=ki, tt=tt:
                          E.matmul(psu[:, :],
                                   wu_t[wb][:, ki * 128:(ki + 1) * 128],
                                   a_t[ki][:, tt * 512:(tt + 1) * 512],
                                   start=(ki == 0), stop=(ki == nK - 1)),
                          mark=(ki == nK - 1))
            p.wait("scalar", gm)
            if idx >= 2:
                p.wait("scalar", edone[idx - 2])
            st = p.op("scalar", lambda E, psg=psg, tt=tt: E.activation(
                out=sg[tt % 2][:, :], in_=psg[:, :], func=AF.Silu), mark=True)
            p.wait("vector", st, um)
            if tt == 0 and sfree[gi % 2] is not None:
                p.wait("vector", sfree[gi % 2])
            edone[idx] = p.op("vector", lambda E, psu=psu, gi=gi, tt=tt:
                              E.tensor_tensor(
                                  out=stg[gi % 2][:, tt * 512:(tt + 1) * 512],
                                  in0=sg[tt % 2][:, :], in1=psu[:, :],
                                  op=ALU.mult), mark=True)
        p.wait("sync", edone[gi * nT + nT - 1])
        sfree[gi % 2] = p.dma("sync", H[gi * 128:(gi + 1) * 128, :],
                              stg[gi % 2][:])
        wfree[wb] = edone[gi * nT + nT - 1]
    p.wait("gpsimd", sfree[0], sfree[1])


def run_projpair(A_bf, Wslabs, Mh):
    K = A_bf[0].shape[0]
    key = f"projpair_{K}_{Mh}"
    nc = get_prog(key, b_projpair, K, Mh)
    res = launch(nc, [{"A": A_bf[c], "W": Wslabs} for c in range(NC)])
    return [np.asarray(res[c]["H"]) for c in range(NC)]



def b_rmssilu(p, eps, Df=D, Tn=T):
    nD = Df // 128
    nT = Tn // 512
    y = p.inp("y", [Df, Tn], F32)
    g = p.outp("g", [Df, Tn], F32)

    yt = [p.sb(f"y{k}", [128, Tn], F32) for k in range(nD)]
    sq = [p.sb(f"sq{j}", [128, 512], BF16) for j in range(2)]
    onesb = p.sb("onesb", [128, 1], BF16)
    onesr = p.sb("onesr", [1, 128], F32)
    rows = p.sb("rows", [1, 1024], F32)
    brd1 = p.sb("brd1", [128, Tn], F32)
    epsc = p.sb("epsc", [1, 1], F32)
    stg = [p.sb(f"s{j}", [128, Tn], F32) for j in range(2)]
    tmp = p.sb("tmp", [128, Tn], F32)
    ps2 = p.ps("ps2", [1, 512])
    psb = p.ps("psb", [128, 512])

    ytoks = [p.dma("sync", yt[k][:], y[k * 128:(k + 1) * 128, :])
             for k in range(nD)]
    m0 = p.op("vector", lambda E: E.memset(epsc[:], float(eps)), mark=True)
    m2 = p.op("vector", lambda E: E.memset(onesb[:], 1.0), mark=True)
    m3 = p.op("vector", lambda E: E.memset(onesr[:], 1.0), mark=True)
    p.wait("scalar", *ytoks)
    p.wait("scalar", m0)
    rowdone = None
    brddone = {}
    for tt in range(nT):
        sl = slice(tt * 512, (tt + 1) * 512)
        s2tok = None
        sqdone = {}
        for k in range(nD):
            if k >= 2:
                p.wait("scalar", sqdone[k - 2])
            sqt = p.op("scalar", lambda E, k=k, sl=sl:
                       E.activation(out=sq[k % 2][:, :], in_=yt[k][:, sl],
                                    func=AF.Square), mark=True)
            p.wait("tensor", sqt)
            if k == 0:
                p.wait("tensor", m2)
                if rowdone is not None:
                    p.wait("tensor", rowdone)
            s2tok = p.op("tensor", lambda E, k=k:
                         E.matmul(ps2[:, :], onesb[:, :], sq[k % 2][:, :],
                                  start=(k == 0), stop=(k == nD - 1)),
                         mark=True)
            sqdone[k] = s2tok
        s2m = rows[:, 0:512]
        rstd = rows[:, 512:1024]
        p.wait("vector", s2tok)
        vtok = p.op("vector", lambda E, s2m=s2m: E.tensor_scalar(
            out=s2m, in0=ps2[:, :], scalar1=1.0 / Df, scalar2=None,
            op0=ALU.mult), mark=True)
        p.wait("scalar", vtok)
        p.op("scalar", lambda E, s2m=s2m, rstd=rstd: E.activation(
            out=rstd, in_=s2m, func=AF.Ln, bias=epsc[:, 0:1]))
        rtok = p.op("scalar", lambda E, rstd=rstd: E.activation(
            out=rstd, in_=rstd, func=AF.Exp, scale=-0.5), mark=True)
        rowdone = rtok
        p.wait("tensor", rtok, m3)
        if tt >= 1:
            p.wait("tensor", brddone[tt - 1])
        bt1 = p.op("tensor", lambda E, rstd=rstd: E.matmul(
            psb[:, :], onesr[:, :], rstd, start=True, stop=True), mark=True)
        p.wait("scalar", bt1)
        e1 = p.op("scalar", lambda E, sl=sl: E.copy(out=brd1[:, sl],
                                                    in_=psb[:, :]), mark=True)
        brddone[tt] = e1
        p.wait("vector", e1)
    p.wait("vector", *ytoks)
    stok = [None, None]
    gt_prev = None
    for k in range(nD):
        j = k % 2
        if gt_prev is not None:
            p.wait("vector", gt_prev)
        vt = p.op("vector", lambda E, k=k: E.tensor_tensor(
            out=tmp[:], in0=yt[k][:], in1=brd1[:], op=ALU.mult), mark=True)
        p.wait("scalar", vt)
        if stok[j] is not None:
            p.wait("scalar", stok[j])
        gt = p.op("scalar", lambda E, j=j: E.activation(
            out=stg[j][:], in_=tmp[:], func=AF.Silu), mark=True)
        gt_prev = gt
        p.wait("sync", gt)
        stok[j] = p.dma("sync", g[k * 128:(k + 1) * 128, :], stg[j][:])
    p.wait("gpsimd", stok[0], stok[1])


def run_rmssilu(Y, eps=1e-6, Df=D):
    key = f"rmssilu_{eps}_{Df}"
    nc = get_prog(key, b_rmssilu, eps, Df)
    res = launch(nc, [{"y": Y[c]} for c in range(NC)])
    return [np.asarray(res[c]["g"]) for c in range(NC)]


def layer_fp4_ffn(xnb, gu_slabs, dn_slabs):
    H = run_projpair(xnb, gu_slabs, 5632)
    Y = run_proj("fp4b", H, dn_slabs, ksplit=2)
    return run_rmssilu(Y, eps=1e-6)
